# version 14
# baseline (speedup 1.0000x reference)
import contextlib
import numpy as np
import ml_dtypes
import concourse.bass as bass
import concourse.mybir as mybir
from concourse.bass_utils import run_bass_kernel_spmd

F32 = mybir.dt.float32
BF16 = mybir.dt.bfloat16
AF = mybir.ActivationFunctionType
ALU = mybir.AluOpType
AXX = mybir.AxisListType.X

N_CORES = 8
T = 4096
D = 1024
NC = 8
TCN = 8
EPS = 1e-5
LN8TH = float(np.log(0.125))


class Sem:
    def __init__(self, handle, name):
        self.h = handle
        self.n = 0
        self.name = name


def build(stage=3, debug_scratch=False, p1_projs=3, p1_tcs=TCN, p1_parts=3):
    nc = bass.Bass("TRN2", target_bir_lowering=False, debug=False)
    ops = {"tensor": [], "scalar": [], "vector": [], "sync": [], "gpsimd": []}
    guards = {}

    def din(name, shape, dt=F32):
        return nc.dram_tensor(name, shape, dt, kind="ExternalInput").ap()

    x_d = din("x", [T, D])
    w1_d = {p: din(f"{p}w1", [D, 4096]) for p in "qkv"}
    w2_d = {p: din(f"{p}w2", [4096, D]) for p in "qkv"}
    b1_d = {p: din(f"{p}b1r", [128, 32]) for p in "qkv"}
    b2_d = {p: din(f"{p}b2r", [128, 8]) for p in "qkv"}
    projw_d = din("projw", [D, D])
    projb_d = din("projb", [1, D])
    winrep_d = din("winrep", [D, 1024])
    w1flat_d = din("w1flat", [128, 8])
    binrep_d = din("binrep", [128, 8])
    b1flat_d = din("b1flat", [128, 8])
    w2bd_d = din("w2bd", [1024, 128])
    fb2_d = din("fb2", [64, 1])
    wout_d = din("wout", [64, D])
    bout_d = din("bout", [1, D])
    ident_d = din("ident", [128, 128], BF16)
    ind2_d = din("ind2", [128, 2], BF16)

    out_d = nc.dram_tensor("out", [T, D], F32, kind="ExternalOutput").ap()
    skind = "ExternalOutput" if debug_scratch else "Internal"
    lnxT_s = nc.dram_tensor("lnxT_s", [D, T], BF16, kind=skind).ap()
    pT_s = {p: nc.dram_tensor(f"{p}T_s", [D, T], BF16, kind=skind).ap() for p in "qkv"}

    st_ = contextlib.ExitStack()
    with st_:
        def sbp(name, shape, dt):
            return st_.enter_context(nc.sbuf_tensor(name, shape, dt))

        ident_sb = sbp("ident_sb", [128, 128], BF16)
        ind2_sb = sbp("ind2_sb", [128, 2], BF16)
        ones1f = sbp("ones1f", [1, 128], F32)
        nmax_q = sbp("nmax_q", [2, 64], F32)
        nmax_k = sbp("nmax_k", [2, 64], F32)
        kmax8 = sbp("kmax8", [2, 8], F32)
        prodm = sbp("prodm", [2, 64], F32)
        negm = sbp("negm", [2, 64], F32)
        negm_row = sbp("negm_row", [1, 128], F32)
        mbias = sbp("mbias", [128, 128], F32)
        stats = sbp("stats", [128, 12], F32)
        mv = sbp("mv", [128, 2], F32)
        veps = sbp("veps", [128, 1], F32)
        rstd = sbp("rstd", [128, 1], F32)
        negmr = sbp("negmr", [128, 1], F32)

        psA = st_.enter_context(nc.psum_tensor("psA", [128, 1024], F32))
        psB = st_.enter_context(nc.psum_tensor("psB", [128, 1024], F32))
        psO = st_.enter_context(nc.psum_tensor("psO", [128, 512], F32))
        psM = st_.enter_context(nc.psum_tensor("psM", [128, 512], F32))
        psTa = st_.enter_context(nc.psum_tensor("psTa", [128, 256], BF16))
        psTb = st_.enter_context(nc.psum_tensor("psTb", [128, 256], BF16))

        st_t = Sem(st_.enter_context(nc.semaphore("st")), "st")
        sa = Sem(st_.enter_context(nc.semaphore("sa")), "sa")
        sv = Sem(st_.enter_context(nc.semaphore("sv")), "sv")
        dy = Sem(st_.enter_context(nc.semaphore("dy")), "dy")
        dg = Sem(st_.enter_context(nc.semaphore("dg")), "dg")
        do = Sem(st_.enter_context(nc.semaphore("do")), "do")
        dc = Sem(st_.enter_context(nc.semaphore("dc")), "dc")
        dn = Sem(st_.enter_context(nc.semaphore("dn")), "dn")
        db = Sem(st_.enter_context(nc.semaphore("db")), "db")
        dl = Sem(st_.enter_context(nc.semaphore("dl")), "dl")
        dls = Sem(st_.enter_context(nc.semaphore("dls")), "dls")
        dp = Sem(st_.enter_context(nc.semaphore("dp")), "dp")
        ds = Sem(st_.enter_context(nc.semaphore("ds")), "ds")
        dg2 = Sem(st_.enter_context(nc.semaphore("dg2")), "dg2")
        dq = Sem(st_.enter_context(nc.semaphore("dq")), "dq")

        ENGINE_SEMS = ("st", "sa", "sv")

        def wait(eng, sem, tgt):
            assert tgt <= sem.n, f"wait on future {sem.name}: {tgt} > {sem.n}"
            if tgt > 0:
                ops[eng].append(("wait", lambda e, h=sem.h, t=tgt: e.wait_ge(h, t), None, 0))

        def step(eng, sem, buildfn, k=1):
            sem.n += k
            if k > 0:
                kind = "inc" if sem.name in ENGINE_SEMS else "op"
                ops[eng].append((kind, lambda e, b=buildfn, h=sem.h, kk=k: b(e).then_inc(h, kk), sem.h, sem.n))
            else:
                ops[eng].append(("op", lambda e, b=buildfn: b(e), None, 0))
            return sem.n

        def op(eng, buildfn):
            ops[eng].append(("op", lambda e, b=buildfn: b(e), None, 0))

        def dma(q, sem, out_ap, in_ap):
            return step(q, sem, lambda e, o=out_ap, i=in_ap: e.dma_start(o, i), 16)

        def guard(eng, key):
            g = guards.get(key)
            if g is not None:
                wait(eng, g[0], g[1])

        own = {"tensor": st_t, "scalar": sa, "vector": sv}

        def rest(eng):
            s = own[eng]
            ops[eng].append(("wait", lambda e, h=s.h, t=s.n: e.wait_ge(h, t), None, 0))

        def free(key, sem, count):
            guards[key] = (sem, count)

        def emit_ln_rstd(src_a, src_b):
            guard("vector", "stats")
            step("vector", sv, lambda e: e.bn_stats(stats[:, 0:6], src_a))
            s2 = step("vector", sv, lambda e: e.bn_stats(stats[:, 6:12], src_b))
            free("stats", sv, s2)
            wait("vector", sv, s2)
            guard("vector", "mv")
            ag = step("vector", sv, lambda e: e.bn_aggr(mv[:], stats[:]))
            wait("vector", sv, ag)
            guard("vector", "veps")
            ve = step("vector", sv, lambda e: e.tensor_scalar(
                veps[:], mv[:, 1:2], EPS, None, op0=ALU.add))
            wait("scalar", sv, ve)
            guard("scalar", "rstd")
            l1 = step("scalar", sa, lambda e: e.activation(rstd[:], veps[:], AF.Ln))
            free("veps", sa, l1)
            wait("scalar", sa, l1)
            l2 = step("scalar", sa, lambda e: e.activation(rstd[:], rstd[:], AF.Exp, scale=-0.5))
            wait("vector", sa, l2)
            guard("vector", "negmr")
            nm = step("vector", sv, lambda e: e.tensor_scalar(
                negmr[:], mv[:, 0:1], rstd[:], -1.0, op0=ALU.mult, op1=ALU.mult))
            free("mv", sv, nm)
            return l2, nm

        dma("gpsimd", dc, ident_sb[:], ident_d)
        dma("gpsimd", dc, ind2_sb[:], ind2_d)
        const_ld = dc.n
        ones_set = step("vector", sv, lambda e: e.memset(ones1f[:], 1.0))

        p1 = contextlib.ExitStack()
        def sb1(name, shape, dt):
            return p1.enter_context(nc.sbuf_tensor(name, shape, dt))

        xa = sb1("xa", [128, 1024], F32)
        lnx_bf = sb1("lnx_bf", [128, 1024], BF16)
        lnxT_sb = sb1("lnxT_sb", [128, 4096], BF16)
        w1sb = sb1("w1sb", [128, 8 * 4096], BF16)
        w2sb = sb1("w2sb", [128, 32 * 1024], BF16)
        b1sb = sb1("b1sb", [128, 32], F32)
        b2sb = sb1("b2sb", [128, 8], F32)
        h_sb = sb1("h_sb", [128, 32 * 512], BF16)
        lt_sb2 = sb1("lt_sb2", [128, 2 * 8 * 512], BF16)
        ost = sb1("ost", [128, 8 * 512], BF16)
        sq_sb = sb1("sq_sb", [128, 2 * 512], BF16)

        for tc in range(TCN):
            for m4 in range(4):
                tt = tc * 4 + m4
                guard("sync", "xa")
                xld = dma("sync", dy, xa[:], x_d[tt * 128:(tt + 1) * 128, :])
                wait("vector", dy, xld)
                _, _ = emit_ln_rstd(xa[:, 0:512], xa[:, 512:1024])
                guard("vector", "lnx_bf")
                nrm = step("vector", sv, lambda e: e.tensor_scalar(
                    lnx_bf[:], xa[:], mv[:, 0:1], rstd[:],
                    op0=ALU.subtract, op1=ALU.mult))
                free("xa", sv, nrm)
                guards["mv"] = (sv, nrm)
                guards["rstd"] = (sv, nrm)
                wait("tensor", sv, nrm)
                if tt == 0:
                    wait("tensor", dc, const_ld)
                for c in range(NC):
                    slot, skey = (psTa, "psTa") if c % 2 == 0 else (psTb, "psTb")
                    guard("tensor", skey)
                    tr = step("tensor", st_t, lambda e, c=c, s=slot: e.transpose(
                        s[:, 0:128], lnx_bf[:, c * 128:(c + 1) * 128], ident_sb[:]))
                    if c >= 6:
                        free("lnx_bf", st_t, tr)
                    wait("vector", st_t, tr)
                    if m4 == 0 and c == 0:
                        guard("vector", "lnxT_sb")
                    cp = step("vector", sv, lambda e, c=c, m4=m4, s=slot: e.tensor_copy(
                        lnxT_sb[:, c * 512 + m4 * 128: c * 512 + m4 * 128 + 128],
                        s[:, 0:128]))
                    free(skey, sv, cp)
            wait("sync", sv, sv.n)
            for c in range(NC):
                dma("sync", dls, lnxT_s[c * 128:(c + 1) * 128, tc * 512:(tc + 1) * 512],
                    lnxT_sb[:, c * 512:(c + 1) * 512])
            free("lnxT_sb", dls, dls.n)

        if stage >= 1:
            wait("sync", dls, dls.n)

            def load_lt(gi, tc):
                par = gi % 2
                guard("sync", f"lt{par}")
                base = par * 8 * 512
                for c in range(NC):
                    dma("sync", dl, lt_sb2[:, base + c * 512: base + (c + 1) * 512],
                        lnxT_s[c * 128:(c + 1) * 128, tc * 512:(tc + 1) * 512])
                return dl.n

            gi_ld = 0
            gi_ld_counts = {}
            plan_chunks = [(p, tc) for p in ["k", "v", "q"][:p1_projs]
                           for tc in range(p1_tcs)]
            for pi, p in enumerate(["k", "v", "q"][:p1_projs]):
                guard("gpsimd", "w12")
                for kc in range(8):
                    dma("gpsimd", dg, w1sb[:, kc * 4096:(kc + 1) * 4096],
                        w1_d[p][kc * 128:(kc + 1) * 128, :])
                w1load = dg.n
                for kc in range(32):
                    dma("gpsimd", dg2, w2sb[:, kc * 1024:(kc + 1) * 1024],
                        w2_d[p][kc * 128:(kc + 1) * 128, :])
                wload = dg2.n
                guard("sync", "b12")
                guard("sync", "b12_a")
                bl = dma("sync", db, b1sb[:], b1_d[p])
                bl = dma("sync", db, b2sb[:], b2_d[p])

                for tc in range(p1_tcs):
                    gi = pi * p1_tcs + tc
                    while gi_ld <= gi + 1 and gi_ld < len(plan_chunks):
                        gi_ld_counts[gi_ld] = load_lt(gi_ld, plan_chunks[gi_ld][1])
                        gi_ld += 1
                    par = gi % 2
                    lt_sb = lt_sb2[:, par * 8 * 512:(par + 1) * 8 * 512]
                    lt_ld = gi_ld_counts[min(gi + 1, len(plan_chunks) - 1)]
                    wait("tensor", dl, lt_ld)
                    if tc == 0:
                        wait("tensor", dg, w1load)
                    for mc in range(32):
                        hkey = "psH0" if mc % 2 == 0 else "psH1"
                        half = psA if mc % 2 == 0 else psB
                        guard("tensor", hkey)
                        for kc in range(8):
                            ra = lt_sb[:, kc * 512:(kc + 1) * 512]
                            step("tensor", st_t, lambda e, mc=mc, kc=kc, h=half, ra=ra: e.matmul(
                                h[:, 0:512],
                                w1sb[:, kc * 4096 + mc * 128: kc * 4096 + mc * 128 + 128],
                                ra,
                                start=(kc == 0), stop=(kc == 7)),
                                k=(1 if kc == 7 else 0))
                        mmd = st_t.n
                        wait("scalar", st_t, mmd)
                        if tc == 0 and mc == 0:
                            wait("scalar", db, bl)
                        guard("scalar", f"h_sb{mc}")
                        ge = step("scalar", sa, lambda e, mc=mc, h=half: e.activation(
                            h_sb[:, mc * 512:(mc + 1) * 512], h[:, 0:512], AF.Gelu,
                            bias=b1sb[:, mc:mc + 1]))
                        free(hkey, sa, ge)
                    pend_norm = []
                    gelu_done = sa.n
                    rest("scalar")
                    if tc == 0:
                        wait("tensor", dg2, wload)
                    wait("tensor", sa, gelu_done)
                    guards[f"lt{par}"] = (st_t, st_t.n)
                    for oc in range(NC if p1_parts >= 2 else 0):
                        okey = "psO0" if oc % 2 == 0 else "psO1"
                        if p1_parts == 6:
                            half = psO if oc % 2 == 0 else psM
                            oslc = (0, 512)
                        else:
                            half = psA if oc % 2 == 0 else psB
                            oslc = (512, 1024)
                        guard("tensor", okey)
                        kcn = 8 if p1_parts in (7, 8, 9) else 32
                        for kc in range(kcn):
                            if p1_parts == 8:
                                rhs_ap = lt_sb[:, kc * 512:(kc + 1) * 512]
                            else:
                                rhs_ap = h_sb[:, kc * 512:(kc + 1) * 512]
                            if p1_parts == 9:
                                lhs_ap = w1sb[:, kc * 4096 + oc * 128: kc * 4096 + oc * 128 + 128]
                            else:
                                lhs_ap = w2sb[:, kc * 1024 + oc * 128: kc * 1024 + oc * 128 + 128]
                            step("tensor", st_t, lambda e, h=half, s=oslc, kcn=kcn,
                                 kc=kc, la=lhs_ap, ra=rhs_ap: e.matmul(
                                h[:, s[0]:s[1]], la, ra,
                                start=(kc == 0), stop=(kc == kcn - 1)),
                                k=(1 if kc == kcn - 1 else 0))
                        omm = st_t.n
                        wait("vector", st_t, omm)
                        if tc == 0 and oc == 0:
                            wait("vector", db, bl)
                        guard("vector", f"ost{oc}")
                        if p1_parts == 4:
                            ev = step("vector", sv, lambda e, oc=oc, h=half, s=oslc: e.tensor_copy(
                                ost[:, oc * 512:(oc + 1) * 512], h[:, s[0]:s[1]]))
                        else:
                            ev = step("vector", sv, lambda e, oc=oc, h=half, s=oslc: e.tensor_scalar(
                                ost[:, oc * 512:(oc + 1) * 512], h[:, s[0]:s[1]],
                                b2sb[:, oc:oc + 1], None, op0=ALU.add))
                        free(okey, sv, ev)
                        if p in ("k", "q") and p1_parts >= 3:
                            sqs = sq_sb[:, (oc % 2) * 512:(oc % 2) * 512 + 512]
                            guard("vector", f"sq{oc % 2}")
                            wait("vector", sv, ev)
                            sq = step("vector", sv, lambda e, s=sqs, oc=oc: e.tensor_tensor(
                                s, ost[:, oc * 512:(oc + 1) * 512],
                                ost[:, oc * 512:(oc + 1) * 512], op=ALU.mult))
                            pend_norm.append((oc, sq, sqs))
                        if pend_norm and (pend_norm[0][1] <= sv.n - 1 or oc == NC - 1):
                            pass
                        while pend_norm and (oc > pend_norm[0][0] or oc == NC - 1):
                            poc, psq, psqs = pend_norm.pop(0)
                            wait("tensor", sv, psq)
                            guard("tensor", "psM")
                            nmm = step("tensor", st_t, lambda e, s=psqs: e.matmul(
                                psM[0:2, 0:512], ind2_sb[:], s, start=True, stop=True))
                            free(f"sq{poc % 2}", st_t, nmm)
                            wait("vector", st_t, nmm)
                            buf = nmax_k if p == "k" else nmax_q
                            idx = poc * 8 + tc
                            rd = step("vector", sv, lambda e, b=buf, i=idx: e.tensor_reduce(
                                b[0:2, i:i + 1], psM[0:2, 0:512], axis=AXX, op=ALU.max))
                            free("psM", sv, rd)
                    for mc in range(32):
                        guards[f"h_sb{mc}"] = (st_t, st_t.n)
                    wait("sync", sv, sv.n)
                    if p1_parts != 5:
                        for oc in range(NC):
                            dma("sync", dls, pT_s[p][oc * 128:(oc + 1) * 128,
                                                     tc * 512:(tc + 1) * 512],
                                ost[:, oc * 512:(oc + 1) * 512])
                        for oc in range(NC):
                            guards[f"ost{oc}"] = (dls, dls.n)
                guards["w12"] = (st_t, st_t.n)
                guards["b12"] = (sv, sv.n)
                guards["b12_a"] = (sa, sa.n)

        if stage < 3 and debug_scratch:
            dbg_d = nc.dram_tensor("dbg", [128, 4096], BF16,
                                   kind="ExternalOutput").ap()
            wait("sync", sv, sv.n)
            wait("sync", sa, sa.n)
            wait("sync", st_t, st_t.n)
            dma("sync", dy, dbg_d, lnxT_sb[:] if stage < 1 else lt_sb2[:, 0:4096])

        p1_t = step("tensor", st_t, lambda e: e.nop())
        p1_a = step("scalar", sa, lambda e: e.nop())
        p1_v = step("vector", sv, lambda e: e.nop())
        p1_dy = dy.n
        p1_dls = dls.n
        p1.close()

        if stage >= 2:
            p2 = contextlib.ExitStack()
            def sb2(name, shape, dt):
                return p2.enter_context(nc.sbuf_tensor(name, shape, dt))

            OT = sb2("OT", [128, 8 * 4096], BF16)
            projw_sb = sb2("projw_sb", [128, 8 * 1024], BF16)
            projb_bc = sb2("projb_bc", [128, 1024], BF16)
            winrep_sb = sb2("winrep_sb", [128, 8 * 1024], BF16)
            w2bd_sb = sb2("w2bd_sb", [128, 8 * 128], BF16)
            wout_sb = sb2("wout_sb", [64, 1024], BF16)
            bout_bc = sb2("bout_bc", [128, 1024], BF16)
            w1f_sb = sb2("w1f_sb", [128, 8], F32)
            binr_sb = sb2("binr_sb", [128, 8], F32)
            b1f_sb = sb2("b1f_sb", [128, 8], F32)
            hb_sb = sb2("hb_sb", [128, 8], F32)
            fb2_sb = sb2("fb2_sb", [64, 1], F32)
            certj = sb2("certj", [128, 1], BF16)
            p2a = contextlib.ExitStack()
            def sba(name, shape, dt):
                return p2a.enter_context(nc.sbuf_tensor(name, shape, dt))
            kT_2 = sba("kT_2", [128, 2 * 4096], BF16)
            vT_2 = sba("vT_2", [128, 2 * 4096], BF16)
            qT_p = sba("qT_p", [128, 4096], BF16)
            vh = {(pp, e2): sba(f"vh{pp}{e2}", [128, 32 * 65 + 64], BF16)
                  for pp in range(2) for e2 in range(2)}
            qz = {0: sba("qz0", [128, 4096], BF16),
                  1: sba("qz1", [128, 4096], BF16)}
            expS = sba("expS", [128, 4 * 1024], BF16)
            o_2 = sba("o_2", [64, 3 * 512], BF16)
            recip_sb = sba("recip_sb", [1, 3 * 512], F32)
            sums_row = sba("sums_row", [1, 3 * 512], F32)
            stage_row = sums_row
            xt_sb = None

            wait("gpsimd", st_t, p1_t)
            wait("gpsimd", sa, p1_a)
            wait("gpsimd", sv, p1_v)
            wait("gpsimd", dy, p1_dy)
            wait("gpsimd", dls, p1_dls)
            for kc in range(8):
                dma("gpsimd", dg, projw_sb[:, kc * 1024:(kc + 1) * 1024],
                    projw_d[kc * 128:(kc + 1) * 128, :])
            for kc in range(8):
                dma("gpsimd", dg, winrep_sb[:, kc * 1024:(kc + 1) * 1024],
                    winrep_d[kc * 128:(kc + 1) * 128, :])
            for kc in range(8):
                dma("gpsimd", dg, w2bd_sb[:, kc * 128:(kc + 1) * 128],
                    w2bd_d[kc * 128:(kc + 1) * 128, :])
            dma("gpsimd", dg, wout_sb[:], wout_d)
            w3_ld = dg.n

            wait("sync", st_t, p1_t)
            wait("sync", sa, p1_a)
            wait("sync", sv, p1_v)
            wait("sync", dls, p1_dls)
            pbl = dma("sync", ds, stage_row[0:1, 0:1024], projb_d)
            dma("sync", ds, w1f_sb[:], w1flat_d)
            dma("sync", ds, binr_sb[:], binrep_d)
            dma("sync", ds, b1f_sb[:], b1flat_d)
            dma("sync", ds, fb2_sb[:], fb2_d)
            setup_ld = ds.n

            wait("vector", sv, p1_v)
            km = step("vector", sv, lambda e: e.tensor_reduce(
                kmax8[0:2, 0:8],
                nmax_k[0:2, :].rearrange("p (c t) -> p c t", t=8),
                axis=AXX, op=ALU.max))
            wait("vector", sv, km)
            q3 = nmax_q[0:2, :].rearrange("p (c t) -> p c t", t=8)
            k3 = kmax8[0:2, 0:8].rearrange("p (c t) -> p c t", t=1)
            q3b, k3b = bass.broadcast_tensor_aps(q3, k3)
            p3o = prodm[0:2, :].rearrange("p (c t) -> p c t", t=8)
            pr_ = step("vector", sv, lambda e: e.tensor_tensor(p3o, q3b, k3b, op=ALU.mult))
            wait("scalar", sv, pr_)
            wait("scalar", sa, p1_a)
            lp = step("scalar", sa, lambda e: e.activation(negm[0:2, :], prodm[0:2, :], AF.Ln))
            wait("scalar", sa, lp)
            exm = step("scalar", sa, lambda e: e.activation(
                negm[0:2, :], negm[0:2, :], AF.Exp, scale=0.5))
            wait("vector", sa, exm)
            ng = step("vector", sv, lambda e: e.tensor_scalar(
                negm[0:2, :], negm[0:2, :], -0.125, None, op0=ALU.mult))
            wait("gpsimd", sv, ng)
            nr = dma("gpsimd", dn, negm_row[0:1, 0:128], negm[0:2, 0:64])
            wait("tensor", dn, nr)
            wait("tensor", sv, ones_set)
            wait("tensor", sa, p1_a)
            wait("tensor", sv, p1_v)
            guard("tensor", "psM")
            mbm = step("tensor", st_t, lambda e: e.matmul(
                psM[0:128, 0:128], ones1f[0:1, :], negm_row[0:1, :],
                start=True, stop=True))
            wait("vector", st_t, mbm)
            mbc = step("vector", sv, lambda e: e.tensor_copy(mbias[:], psM[0:128, 0:128]))
            free("psM", sv, mbc)

            wait("tensor", ds, setup_ld)
            for dst in (projb_bc, bout_bc):
                for half in range(2):
                    guard("tensor", "psM")
                    r1 = step("tensor", st_t, lambda e, h=half: e.matmul(
                        psM[:, 0:512], ones1f[0:1, :],
                        stage_row[0:1, h * 512:(h + 1) * 512], start=True, stop=True))
                    wait("vector", st_t, r1)
                    cpb = step("vector", sv, lambda e, d=dst, h=half: e.tensor_copy(
                        d[:, h * 512:(h + 1) * 512], psM[:, 0:512]))
                    free("psM", sv, cpb)
                if dst is projb_bc:
                    wait("sync", st_t, st_t.n)
                    bol = dma("sync", db, stage_row[0:1, 0:1024], bout_d)
                    wait("tensor", db, bol)
                else:
                    guards["sums_0"] = (st_t, st_t.n)
                    guards["sums_1"] = (st_t, st_t.n)
            wait("vector", ds, setup_ld)
            hb1 = step("vector", sv, lambda e: e.tensor_tensor(
                hb_sb[:], w1f_sb[:], binr_sb[:], op=ALU.mult))
            wait("vector", sv, hb1)
            hb2 = step("vector", sv, lambda e: e.tensor_tensor(
                hb_sb[:], hb_sb[:], b1f_sb[:], op=ALU.add))
            for key in vh:
                step("vector", sv, lambda e, b=vh[key]: e.memset(b[:], 1.0))
            for e_ in range(2):
                step("vector", sv, lambda e, b=qz[e_]: e.memset(b[:], 0.0))
            vh_ones = sv.n

            def warm_burst(n=20):
                guard("tensor", "psM")
                for _ in range(n):
                    op("tensor", lambda e: e.matmul(
                        psM[64:128, 0:512], ident_sb[0:64, 0:64],
                        projb_bc[0:64, 0:512], start=True, stop=True))

            def load_kv(c):
                par = c % 2
                guard("sync", f"kv{par}")
                if c >= 2:
                    wait("sync", dp, kv_counts[c - 1])

                dma("sync", dp, kT_2[:, par * 4096:(par + 1) * 4096],
                    pT_s["k"][c * 128:(c + 1) * 128, :])
                dma("sync", dp, vT_2[:, par * 4096:(par + 1) * 4096],
                    pT_s["v"][c * 128:(c + 1) * 128, :])
                return dp.n

            wait("sync", dls, p1_dls)
            rc_counts = {}
            scp_counts = {}
            meta = {}
            kv_counts = {}
            kv_counts[0] = load_kv(0)
            qld = 0
            for c in range(NC):
                par = c % 2
                kT_p = kT_2[:, par * 4096:(par + 1) * 4096]
                vT_p = vT_2[:, par * 4096:(par + 1) * 4096]
                guard("sync", "pairbuf")
                if c >= 1:
                    wait("sync", dq, qld)
                qld = dma("sync", dq, qT_p[:], pT_s["q"][c * 128:(c + 1) * 128, :])
                wait("tensor", dp, kv_counts[c])
                wait("tensor", dq, qld)
                if c == 0:
                    wait("tensor", sv, vh_ones)
                def emit_vtr(cc, e2, kc, first_guard=False):
                    slot, skey = (psTa, "psTa") if e2 == 0 else (psTb, "psTb")
                    b2 = e2 * 64
                    vTcc = vT_2[:, (cc % 2) * 4096:((cc % 2) + 1) * 4096]
                    guard("tensor", skey)
                    va = vTcc[b2:b2 + 64, kc * 128:(kc + 1) * 128]
                    ia = ident_sb[b2:b2 + 64, b2:b2 + 64]
                    tr = step("tensor", st_t, lambda e, s=slot, va=va, ia=ia: e.transpose(
                        s[:, 0:64], va, ia))
                    wait("vector", st_t, tr)
                    if first_guard:
                        guard("vector", f"vhfree{cc % 2}")
                    cp = step("vector", sv, lambda e, k2=(cc % 2, e2), kc=kc,
                              s=slot: e.tensor_copy(
                        vh[k2][:, kc * 65: kc * 65 + 64], s[:, 0:64]))
                    free(skey, sv, cp)
                    return tr

                tr_first = None
                if c == 0:
                    for e_ in range(2):
                        for kc in range(32):
                            tr = emit_vtr(0, e_, kc, first_guard=(e_ == 0 and kc == 0))
                            if tr_first is None:
                                tr_first = tr
                pend_vtr = [(c + 1, e2, kc) for e2 in range(2)
                            for kc in range(32)] if c + 1 < NC else []
                wait("vector", dq, qld)
                guard("vector", "qz")
                step("vector", sv, lambda e: e.tensor_copy(
                    qz[0][0:64, :], qT_p[0:64, :]))
                qzdone = step("vector", sv, lambda e: e.tensor_copy(
                    qz[1][64:128, :], qT_p[64:128, :]))
                vtr_done = sv.n
                if c + 1 < NC:
                    wait("sync", st_t, st_t.n if tr_first is None else tr_first)
                    kv_counts[c + 1] = load_kv(c + 1)
                wait("tensor", sv, vtr_done)
                wait("scalar", ds, setup_ld)
                wait("scalar", sv, mbc)

                for qc in range(TCN):
                    for e_ in range(2):
                        base = e_ * 64
                        midx = e_ * 64 + c * 8 + qc
                        it = (c * TCN + qc) * 2 + e_
                        crt_counts = {}
                        for kk in range(16):
                            sbank, bkey = (psA, "psSA") if kk % 2 == 0 else (psB, "psSB")
                            eslot = kk % 4
                            guard("tensor", bkey)
                            for j in range(2):
                                kc = kk * 2 + j
                                ka = kT_p[:, kc * 128:(kc + 1) * 128]
                                qa = qz[e_][:, qc * 512:(qc + 1) * 512]
                                step("tensor", st_t, lambda e, j=j, s=sbank,
                                     ka=ka, qa=qa: e.matmul(
                                    s[:, j * 512:(j + 1) * 512], ka, qa,
                                    start=True, stop=True), k=(1 if j == 1 else 0))
                            smm = st_t.n
                            if pend_vtr and qc >= 6:
                                if pend_vtr[0][1] == 0 and pend_vtr[0][2] == 0:
                                    wait("tensor", dp, kv_counts[c + 1])
                                cc2, e2, kc2 = pend_vtr.pop(0)
                                emit_vtr(cc2, e2, kc2,
                                         first_guard=(e2 == 0 and kc2 == 0))
                            if kk == 3 and it > 1:
                                par_p = (it - 2) % 3
                                guard("tensor", "psM")
                                wait("tensor", sv, rc_counts[par_p])
                                bcm = step("tensor", st_t, lambda e, pp=par_p: e.matmul(
                                    psM[0:64, 0:512], ones1f[0:1, 0:64],
                                    recip_sb[0:1, pp * 512:(pp + 1) * 512],
                                    start=True, stop=True))
                                free(f"rc_{par_p}", st_t, bcm)
                                bcm_prev = bcm
                            if kk > 0:
                                wait("tensor", sv, crt_counts[kk - 1])
                                if kk == 1:
                                    guard("tensor", "psO_attn")
                                for j in range(2):
                                    kc = (kk - 1) * 2 + j
                                    es = (kk - 1) % 4
                                    step("tensor", st_t, lambda e, kc=kc, j=j, es=es,
                                         k2=(c % 2, e_): e.matmul(
                                        psO[0:128, 0:512],
                                        vh[k2][:, kc * 65: kc * 65 + 128],
                                        expS[:, es * 1024 + j * 512:
                                             es * 1024 + (j + 1) * 512],
                                        start=(kc == 0), stop=(kc == 31)),
                                        k=(1 if j == 1 else 0))
                                free(f"expS{(kk - 1) % 4}", st_t, st_t.n)
                            wait("scalar", st_t, smm)
                            guard("scalar", f"expS{eslot}")
                            exq = step("scalar", sa, lambda e, s=sbank, es=eslot,
                                       m=midx: e.activation(
                                expS[:, es * 1024:(es + 1) * 1024], s[:, 0:1024],
                                AF.Exp, bias=mbias[:, m:m + 1], scale=0.125))
                            free(bkey, sa, exq)
                            wait("vector", sa, exq)
                            guard("vector", "certj")
                            crt_counts[kk] = step("vector", sv, lambda e, es=eslot: e.tensor_copy(
                                certj[:], expS[:, es * 1024 + 1023: es * 1024 + 1024]))
                            free("certj", sv, crt_counts[kk])
                            if kk == 6 and it > 0:
                                pr = (it - 1) % 3
                                wait("vector", sv, scp_counts[pr])
                                guard("vector", "rcscr")
                                guard("vector", f"rc_{pr}")
                                rc_counts[pr] = step("vector", sv, lambda e, pr=pr: e.reciprocal(
                                    recip_sb[0:1, pr * 512:(pr + 1) * 512],
                                    sums_row[0:1, pr * 512:(pr + 1) * 512]))
                                free("rcscr", sv, rc_counts[pr])
                                free(f"sums_{pr}", sv, rc_counts[pr])
                            if kk == 9 and it > 1:
                                par_p = (it - 2) % 3
                                wait("vector", st_t, bcm_prev)
                                pm = meta[(it - 2) % 3]
                                mul = step("vector", sv, lambda e, pm=pm, pp=par_p: e.tensor_tensor(
                                    OT[pm[0]:pm[0] + 64,
                                       pm[1] * 4096 + pm[2] * 512: pm[1] * 4096 + (pm[2] + 1) * 512],
                                    o_2[:, pp * 512:(pp + 1) * 512],
                                    psM[0:64, 0:512], op=ALU.mult))
                                free("psM", sv, mul)
                                free(f"o2_{par_p}", sv, mul)
                        wait("tensor", sv, crt_counts[15])
                        for j in range(2):
                            kc = 15 * 2 + j
                            step("tensor", st_t, lambda e, kc=kc, j=j, k2=(c % 2, e_): e.matmul(
                                psO[0:128, 0:512],
                                vh[k2][:, kc * 65: kc * 65 + 128],
                                expS[:, 3 * 1024 + j * 512: 3 * 1024 + (j + 1) * 512],
                                start=False, stop=(j == 1)), k=(1 if j == 1 else 0))
                        free("expS3", st_t, st_t.n)
                        pv_done = st_t.n
                        par = it % 3
                        wait("vector", st_t, pv_done)
                        guard("vector", f"o2_{par}")
                        ocp = step("vector", sv, lambda e, par=par: e.tensor_copy(
                            o_2[:, par * 512:(par + 1) * 512], psO[0:64, 0:512]))
                        guard("vector", f"sums_{par}")
                        scp = step("vector", sv, lambda e, par=par: e.tensor_copy(
                            sums_row[0:1, par * 512:(par + 1) * 512], psO[64:65, 0:512]))
                        free("psO_attn", sv, scp)
                        scp_counts[par] = scp
                        meta[it % 3] = (base, c, qc)
                assert not pend_vtr
                guards["pairbuf"] = (st_t, st_t.n)
                guards[f"vhfree{par}"] = (st_t, st_t.n)
                guards[f"kv{par}"] = (st_t, st_t.n)
                guards["qz"] = (st_t, st_t.n)

            n_it = NC * TCN * 2
            prf = (n_it - 1) % 3
            wait("vector", sv, scp_counts[prf])
            guard("vector", "rcscr")
            rc_counts[prf] = step("vector", sv, lambda e, pr=prf: e.reciprocal(
                recip_sb[0:1, pr * 512:(pr + 1) * 512],
                sums_row[0:1, pr * 512:(pr + 1) * 512]))
            free("rcscr", sv, rc_counts[prf])
            for itf in (n_it - 2, n_it - 1):
                par_p = itf % 3
                guard("tensor", "psM")
                wait("tensor", sv, rc_counts[par_p])
                bcm = step("tensor", st_t, lambda e, pp=par_p: e.matmul(
                    psM[0:64, 0:512], ones1f[0:1, 0:64],
                    recip_sb[0:1, pp * 512:(pp + 1) * 512], start=True, stop=True))
                wait("vector", st_t, bcm)
                pm = meta[itf % 3]
                mul = step("vector", sv, lambda e, pm=pm, pp=par_p: e.tensor_tensor(
                    OT[pm[0]:pm[0] + 64,
                       pm[1] * 4096 + pm[2] * 512: pm[1] * 4096 + (pm[2] + 1) * 512],
                    o_2[:, pp * 512:(pp + 1) * 512],
                    psM[0:64, 0:512], op=ALU.mult))
                free("psM", sv, mul)
                free(f"o2_{par_p}", sv, mul)

            attn_v = sv.n
            attn_t = st_t.n
            attn_a = sa.n
            attn_dp = dp.n
            p2a.close()

            xt_sb = sb2("xt_sb", [128, 1024], F32)
            attn_sb = sb2("attn_sb", [128, 1024], F32)
            x2_sb = sb2("x2_sb", [128, 8 * 1024], F32)
            ln3_bf = sb2("ln3_bf", [128, 1024], BF16)
            ln3T_sb = sb2("ln3T_sb", [128, 2 * 8 * 512], BF16)
            onesbf = sb2("onesbf", [1, 128], BF16)
            hf_sb = sb2("hf_sb", [128, 8 * 512], BF16)
            g_sb = sb2("g_sb", [64, 512], BF16)
            f_sb = sb2("f_sb", [128, 1024], F32)
            out_sb = sb2("out_sb", [128, 1024], F32)

            if stage >= 3:
                wait("tensor", sv, attn_v)
                wait("tensor", sa, attn_a)
                wait("tensor", dg, w3_ld)
                wait("scalar", st_t, attn_t)
                wait("scalar", sv, attn_v)
                wait("sync", st_t, attn_t)
                wait("sync", sv, attn_v)
                wait("sync", sa, attn_a)
                wait("sync", dp, attn_dp)
                out_count = 0
                pending = None
                prj = {}
                ob = step("vector", sv, lambda e: e.memset(onesbf[:], 1.0))
                wait("tensor", sv, ob)

                def proj_mms(t):
                    guard("tensor", "psA3")
                    for nn2 in range(2):
                        for kc in range(8):
                            step("tensor", st_t, lambda e, nn2=nn2, kc=kc, t=t: e.matmul(
                                psA[:, nn2 * 512:(nn2 + 1) * 512],
                                OT[:, kc * 4096 + t * 128: kc * 4096 + (t + 1) * 128],
                                projw_sb[:, kc * 1024 + nn2 * 512:
                                         kc * 1024 + (nn2 + 1) * 512],
                                start=(kc == 0), stop=False), k=0)
                        step("tensor", st_t, lambda e, nn2=nn2: e.matmul(
                            psA[:, nn2 * 512:(nn2 + 1) * 512],
                            onesbf[0:1, 0:128],
                            projb_bc[0:1, nn2 * 512:(nn2 + 1) * 512],
                            start=False, stop=True), k=(1 if nn2 == 1 else 0))
                    prj[t] = st_t.n

                def srep_pair(k, mis):
                    par = k % 2
                    if mis[0] == 0:
                        wait("tensor", sv, pending[1])
                    for mi in mis:
                        hh = mi % 2
                        guard("tensor", f"psF{hh}")
                        for kc in range(8):
                            step("tensor", st_t, lambda e, mi=mi, kc=kc, hh=hh, par=par: e.matmul(
                                psB[:, hh * 512:(hh + 1) * 512],
                                winrep_sb[:, kc * 1024 + mi * 128:
                                          kc * 1024 + mi * 128 + 128],
                                ln3T_sb[:, par * 4096 + kc * 512:
                                        par * 4096 + (kc + 1) * 512],
                                start=(kc == 0), stop=(kc == 7)),
                                k=(1 if kc == 7 else 0))
                    if mis[-1] == 7:
                        guards[f"ln3Tw{par}"] = (st_t, st_t.n)

                def srep_relu(k, mis):
                    if k == 0 and mis[0] == 0:
                        wait("scalar", ds, setup_ld)
                        wait("scalar", sv, hb2)
                    for mi in mis:
                        hh = mi % 2
                        wait("scalar", st_t, srep_done[mi])
                        guard("scalar", f"hf{mi}")
                        rl = step("scalar", sa, lambda e, mi=mi, hh=hh: e.activation(
                            hf_sb[:, mi * 512:(mi + 1) * 512],
                            psB[:, hh * 512:(hh + 1) * 512], AF.Relu,
                            scale=w1f_sb[:, mi:mi + 1], bias=hb_sb[:, mi:mi + 1]))
                        free(f"psF{hh}", sa, rl)
                    if mis[-1] == 7:
                        guards["relu_done"] = (sa, sa.n)

                def g_mms():
                    g = guards["relu_done"]
                    wait("tensor", g[0], g[1])
                    guard("tensor", "psOM4")
                    guard("tensor", "psO_g")
                    for mi in range(8):
                        step("tensor", st_t, lambda e, mi=mi: e.matmul(
                            psO[:, 0:512], w2bd_sb[:, mi * 128:(mi + 1) * 128],
                            hf_sb[:, mi * 512:(mi + 1) * 512],
                            start=(mi == 0), stop=(mi == 7)),
                            k=(1 if mi == 7 else 0))
                    gmm = st_t.n
                    for mi in range(8):
                        guards[f"hf{mi}"] = (st_t, gmm)
                    wait("scalar", st_t, gmm)
                    guard("scalar", "g_sb")
                    gv = step("scalar", sa, lambda e: e.activation(
                        g_sb[:], psO[0:64, 0:512], AF.Identity, bias=fb2_sb[:]))
                    free("psO_g", sa, gv)

                def g_cert():
                    wait("vector", sa, sa.n)
                    guard("vector", "certj")
                    gcrt = step("vector", sv, lambda e: e.tensor_copy(
                        certj[0:64, 0:1], g_sb[:, 511:512]))
                    free("certj", sv, gcrt)
                    guards["gcrt"] = (sv, gcrt)

                def wout_chain(k, j2):
                    nonlocal out_count
                    tt = k * 4 + j2
                    gc = guards["gcrt"]
                    wait("tensor", gc[0], gc[1])
                    guard("tensor", "psOM4")
                    for nn2 in range(2):
                        half = psO if nn2 == 0 else psM
                        step("tensor", st_t, lambda e, j2=j2, nn2=nn2, h=half: e.matmul(
                            h[:, 0:512],
                            g_sb[:, j2 * 128:(j2 + 1) * 128],
                            wout_sb[0:64, nn2 * 512:(nn2 + 1) * 512],
                            start=True, stop=False), k=0)
                        step("tensor", st_t, lambda e, nn2=nn2, h=half: e.matmul(
                            h[:, 0:512],
                            onesbf[0:1, 0:128],
                            bout_bc[0:1, nn2 * 512:(nn2 + 1) * 512],
                            start=False, stop=True), k=(1 if nn2 == 1 else 0))
                    fmm = st_t.n
                    if j2 == 3:
                        guards["g_sb"] = (st_t, fmm)
                    wait("scalar", st_t, fmm)
                    guard("scalar", "f_sb")
                    for nn2 in range(2):
                        half = psO if nn2 == 0 else psM
                        fs = step("scalar", sa, lambda e, nn2=nn2, h=half: e.activation(
                            f_sb[:, nn2 * 512:(nn2 + 1) * 512],
                            h[:, 0:512], AF.Identity))
                    free("psOM4", sa, fs)
                    wait("vector", sa, fs)
                    emit_ln_rstd(f_sb[:, 0:512], f_sb[:, 512:1024])
                    nm4 = sv.n
                    wait("scalar", sv, nm4)
                    guard("scalar", "attn_sb")
                    t4 = step("scalar", sa, lambda e: e.activation(
                        attn_sb[:], f_sb[:], AF.Identity,
                        scale=rstd[:, 0:1], bias=negmr[:, 0:1]))
                    free("f_sb", sa, t4)
                    guards["negmr"] = (sa, t4)
                    guards["rstd"] = (sa, t4)
                    wait("vector", sa, t4)
                    guard("vector", "out_sb")
                    slot = tt % 8
                    fin = step("vector", sv, lambda e, slot=slot: e.tensor_tensor(
                        out_sb[:], attn_sb[:],
                        x2_sb[:, slot * 1024:(slot + 1) * 1024], op=ALU.add))
                    guards["attn_sb"] = (sv, fin)
                    guards[f"x2_{slot}"] = (sv, fin)
                    wait("sync", sv, fin)
                    dma("sync", do, out_d[tt * 128:(tt + 1) * 128, :], out_sb[:])
                    out_count += 1
                    guards["out_sb"] = (do, out_count * 16)

                proj_mms(0)
                srep_done = {}
                for t in range(32):
                    c, j = divmod(t, 4)
                    ffn = pending is not None
                    guard("sync", "xt_sb")
                    xld = dma("sync", dy, xt_sb[:], x_d[t * 128:(t + 1) * 128, :])
                    wait("scalar", st_t, prj[t])
                    guard("scalar", "attn_sb")
                    for nn2 in range(2):
                        pa = step("scalar", sa, lambda e, nn2=nn2: e.activation(
                            attn_sb[:, nn2 * 512:(nn2 + 1) * 512],
                            psA[:, nn2 * 512:(nn2 + 1) * 512], AF.Identity))
                    free("psA3", sa, pa)
                    if t + 1 < 32:
                        proj_mms(t + 1)
                    wait("vector", sa, pa)
                    emit_ln_rstd(attn_sb[:, 0:512], attn_sb[:, 512:1024])
                    nm2 = sv.n
                    wait("scalar", sv, nm2)
                    guard("scalar", "f_sb")
                    t2 = step("scalar", sa, lambda e: e.activation(
                        f_sb[:], attn_sb[:], AF.Identity,
                        scale=rstd[:, 0:1], bias=negmr[:, 0:1]))
                    guards["negmr"] = (sa, t2)
                    guards["rstd"] = (sa, t2)
                    guards["attn_sb"] = (sa, t2)
                    if ffn:
                        if j == 0:
                            srep_pair(pending[0], (0, 1))
                            srep_done[0] = srep_done[1] = st_t.n
                            srep_relu(pending[0], (0, 1))
                        elif j == 1:
                            srep_pair(pending[0], (4, 5))
                            srep_done[4] = srep_done[5] = st_t.n
                            srep_relu(pending[0], (4, 5))
                        elif j == 2:
                            g_mms()
                    wait("vector", sa, t2)
                    wait("vector", dy, xld)
                    guard("vector", f"x2_{t % 8}")
                    x2w = step("vector", sv, lambda e, s=t % 8: e.tensor_tensor(
                        x2_sb[:, s * 1024:(s + 1) * 1024], f_sb[:],
                        xt_sb[:], op=ALU.add))
                    free("xt_sb", sv, x2w)
                    free("f_sb", sv, x2w)
                    wait("vector", sv, x2w)
                    _, nm3 = emit_ln_rstd(x2_sb[:, (t % 8) * 1024: (t % 8) * 1024 + 512],
                                 x2_sb[:, (t % 8) * 1024 + 512: (t % 8) * 1024 + 1024])
                    wait("vector", sv, nm3)
                    guard("vector", "ln3_bf")
                    n3 = step("vector", sv, lambda e, s=t % 8: e.tensor_scalar(
                        ln3_bf[:], x2_sb[:, s * 1024:(s + 1) * 1024],
                        rstd[:, 0:1], negmr[:, 0:1], op0=ALU.mult, op1=ALU.add))
                    guards["negmr"] = (sv, n3)
                    guards["rstd"] = (sv, n3)
                    wait("tensor", sv, n3)
                    for c2 in range(NC):
                        slot2, skey = (psTa, "psTa") if c2 % 2 == 0 else (psTb, "psTb")
                        guard("tensor", skey)
                        tr3 = step("tensor", st_t, lambda e, c2=c2, s=slot2: e.transpose(
                            s[:, 128:256], ln3_bf[:, c2 * 128:(c2 + 1) * 128],
                            ident_sb[:]))
                        if c2 >= 6:
                            free("ln3_bf", st_t, tr3)
                        wait("vector", st_t, tr3)
                        if j == 0 and c2 == 0:
                            guard("vector", f"ln3Tw{c % 2}")
                        cp3 = step("vector", sv, lambda e, c2=c2, j=j, s=slot2, par=c % 2: e.tensor_copy(
                            ln3T_sb[:, par * 4096 + c2 * 512 + j * 128:
                                    par * 4096 + c2 * 512 + j * 128 + 128],
                            s[:, 128:256]))
                        free(skey, sv, cp3)
                    if ffn:
                        if j == 0:
                            srep_pair(pending[0], (2, 3))
                            srep_done[2] = srep_done[3] = st_t.n
                            srep_relu(pending[0], (2, 3))
                        elif j == 1:
                            srep_pair(pending[0], (6, 7))
                            srep_done[6] = srep_done[7] = st_t.n
                            srep_relu(pending[0], (6, 7))
                        elif j == 2:
                            g_cert()
                            wout_chain(pending[0], 0)
                            wout_chain(pending[0], 1)
                        elif j == 3:
                            wout_chain(pending[0], 2)
                            wout_chain(pending[0], 3)
                    if j == 3:
                        pending = (c, sv.n)
                for pr_ in ((0, 1), (2, 3), (4, 5), (6, 7)):
                    srep_pair(pending[0], pr_)
                    for mi in pr_:
                        srep_done[mi] = st_t.n
                    srep_relu(pending[0], pr_)
                g_mms()
                g_cert()
                for j2 in range(4):
                    wout_chain(pending[0], j2)
                wait("sync", do, out_count * 16)
            p2.close()

        if stage < 3:
            wait("sync", st_t, st_t.n)
            wait("sync", sa, sa.n)
            wait("sync", sv, sv.n)
            for s_ in (dy, dg, dg2, dc, dn, db, dl, dls, dp, dq, ds):
                wait("sync", s_, s_.n)

        block = st_.enter_context(nc.Block())

        def replay(eng, lst):
            n = len(lst)
            for i, (kind, f, semh, cnt) in enumerate(lst):
                f(eng)
                if kind == "inc" and (i + 1 >= n or lst[i + 1][0] != "wait"):
                    eng.wait_ge(semh, cnt)

        @block.tensor
        def _(eng):
            replay(eng, ops["tensor"])

        @block.scalar
        def _(eng):
            replay(eng, ops["scalar"])

        @block.vector
        def _(eng):
            replay(eng, ops["vector"])

        @block.sync
        def _(eng):
            replay(eng, ops["sync"])

        @block.gpsimd
        def _(eng):
            replay(eng, ops["gpsimd"])

    return nc


def _host_common(kw):
    f = lambda a: np.ascontiguousarray(np.asarray(a, dtype=np.float32))
    pr = lambda b, k: np.ascontiguousarray(f(b).reshape(k, 128).T)
    common = {
        "qw1": f(kw["L0_attn_q_w1"]), "qw2": f(kw["L0_attn_q_w2"]),
        "kw1": f(kw["L0_attn_k_w1"]), "kw2": f(kw["L0_attn_k_w2"]),
        "vw1": f(kw["L0_attn_v_w1"]), "vw2": f(kw["L0_attn_v_w2"]),
        "qb1r": pr(kw["L0_attn_q_b1"], 32), "qb2r": pr(kw["L0_attn_q_b2"], 8),
        "kb1r": pr(kw["L0_attn_k_b1"], 32), "kb2r": pr(kw["L0_attn_k_b2"], 8),
        "vb1r": pr(kw["L0_attn_v_b1"], 32), "vb2r": pr(kw["L0_attn_v_b2"], 8),
        "projw": f(kw["L0_attn_proj_w"]),
        "projb": f(kw["L0_attn_proj_b"]).reshape(1, 1024),
        "winrep": np.ascontiguousarray(np.repeat(f(kw["L0_ffn_Win"]), 16, axis=1)),
        "w1flat": pr(np.asarray(kw["L0_ffn_W1"]).reshape(1024), 8),
        "binrep": pr(np.repeat(f(kw["L0_ffn_bin_"]), 16), 8),
        "b1flat": pr(np.asarray(kw["L0_ffn_b1"]).reshape(1024), 8),
        "fb2": f(kw["L0_ffn_b2"]).reshape(64, 1),
        "wout": f(kw["L0_ffn_Wout"]),
        "bout": f(kw["L0_ffn_bout"]).reshape(1, 1024),
        "ident": np.eye(128, dtype=np.float32).astype(ml_dtypes.bfloat16),
    }
    w2bd = np.zeros((1024, 128), np.float32)
    w2bd[np.arange(1024), np.arange(1024) // 16] = f(kw["L0_ffn_W2"]).reshape(1024)
    common["w2bd"] = w2bd
    ind2 = np.zeros((128, 2), np.float32)
    ind2[0:64, 0] = 1.0
    ind2[64:128, 1] = 1.0
    common["ind2"] = ind2.astype(ml_dtypes.bfloat16)
    return common


def kernel(L0_attn_x, **kw):
    X = np.ascontiguousarray(np.asarray(L0_attn_x, dtype=np.float32))
    common = _host_common(kw)
    nc = build()
    in_maps = [dict(common, x=np.ascontiguousarray(X[b])) for b in range(N_CORES)]
    res = run_bass_kernel_spmd(nc, in_maps, core_ids=list(range(N_CORES)))
    return np.stack([np.asarray(res.results[b]["out"]) for b in range(N_CORES)], axis=0)


# version 15
# speedup vs baseline: 1.0061x; 1.0061x over previous
import contextlib
import numpy as np
import ml_dtypes
import concourse.bass as bass
import concourse.mybir as mybir
from concourse.bass_utils import run_bass_kernel_spmd

F32 = mybir.dt.float32
BF16 = mybir.dt.bfloat16
AF = mybir.ActivationFunctionType
ALU = mybir.AluOpType
AXX = mybir.AxisListType.X

N_CORES = 8
T = 4096
D = 1024
NC = 8
TCN = 8
EPS = 1e-5
LN8TH = float(np.log(0.125))


class Sem:
    def __init__(self, handle, name):
        self.h = handle
        self.n = 0
        self.name = name


def build(stage=3, debug_scratch=False, p1_projs=3, p1_tcs=TCN, p1_parts=3):
    nc = bass.Bass("TRN2", target_bir_lowering=False, debug=False)
    ops = {"tensor": [], "scalar": [], "vector": [], "sync": [], "gpsimd": []}
    guards = {}

    def din(name, shape, dt=F32):
        return nc.dram_tensor(name, shape, dt, kind="ExternalInput").ap()

    x_d = din("x", [T, D])
    w1_d = {p: din(f"{p}w1", [D, 4096]) for p in "qkv"}
    w2_d = {p: din(f"{p}w2", [4096, D]) for p in "qkv"}
    b1_d = {p: din(f"{p}b1r", [128, 32]) for p in "qkv"}
    b2_d = {p: din(f"{p}b2r", [128, 8]) for p in "qkv"}
    projw_d = din("projw", [D, D])
    projb_d = din("projb", [1, D])
    winrep_d = din("winrep", [D, 1024])
    w1flat_d = din("w1flat", [128, 8])
    binrep_d = din("binrep", [128, 8])
    b1flat_d = din("b1flat", [128, 8])
    w2bd_d = din("w2bd", [1024, 128])
    fb2_d = din("fb2", [64, 1])
    wout_d = din("wout", [64, D])
    bout_d = din("bout", [1, D])
    ident_d = din("ident", [128, 128], BF16)
    ind2_d = din("ind2", [128, 2], BF16)

    out_d = nc.dram_tensor("out", [T, D], F32, kind="ExternalOutput").ap()
    skind = "ExternalOutput" if debug_scratch else "Internal"
    lnxT_s = nc.dram_tensor("lnxT_s", [D, T], BF16, kind=skind).ap()
    pT_s = {p: nc.dram_tensor(f"{p}T_s", [D, T], BF16, kind=skind).ap() for p in "qkv"}

    st_ = contextlib.ExitStack()
    with st_:
        def sbp(name, shape, dt):
            return st_.enter_context(nc.sbuf_tensor(name, shape, dt))

        ident_sb = sbp("ident_sb", [128, 128], BF16)
        ind2_sb = sbp("ind2_sb", [128, 2], BF16)
        ones1f = sbp("ones1f", [1, 128], F32)
        nmax_q = sbp("nmax_q", [2, 64], F32)
        nmax_k = sbp("nmax_k", [2, 64], F32)
        kmax8 = sbp("kmax8", [2, 8], F32)
        prodm = sbp("prodm", [2, 64], F32)
        negm = sbp("negm", [2, 64], F32)
        negm_row = sbp("negm_row", [1, 128], F32)
        mbias = sbp("mbias", [128, 128], F32)
        stats = sbp("stats", [128, 12], F32)
        mv = sbp("mv", [128, 2], F32)
        veps = sbp("veps", [128, 1], F32)
        rstd = sbp("rstd", [128, 1], F32)
        negmr = sbp("negmr", [128, 1], F32)

        psA = st_.enter_context(nc.psum_tensor("psA", [128, 1024], F32))
        psB = st_.enter_context(nc.psum_tensor("psB", [128, 1024], F32))
        psO = st_.enter_context(nc.psum_tensor("psO", [128, 512], F32))
        psM = st_.enter_context(nc.psum_tensor("psM", [128, 512], F32))
        psTa = st_.enter_context(nc.psum_tensor("psTa", [128, 256], BF16))
        psTb = st_.enter_context(nc.psum_tensor("psTb", [128, 256], BF16))

        st_t = Sem(st_.enter_context(nc.semaphore("st")), "st")
        sa = Sem(st_.enter_context(nc.semaphore("sa")), "sa")
        sv = Sem(st_.enter_context(nc.semaphore("sv")), "sv")
        dy = Sem(st_.enter_context(nc.semaphore("dy")), "dy")
        dg = Sem(st_.enter_context(nc.semaphore("dg")), "dg")
        do = Sem(st_.enter_context(nc.semaphore("do")), "do")
        dc = Sem(st_.enter_context(nc.semaphore("dc")), "dc")
        dn = Sem(st_.enter_context(nc.semaphore("dn")), "dn")
        db = Sem(st_.enter_context(nc.semaphore("db")), "db")
        dl = Sem(st_.enter_context(nc.semaphore("dl")), "dl")
        dls = Sem(st_.enter_context(nc.semaphore("dls")), "dls")
        dp = Sem(st_.enter_context(nc.semaphore("dp")), "dp")
        ds = Sem(st_.enter_context(nc.semaphore("ds")), "ds")
        dg2 = Sem(st_.enter_context(nc.semaphore("dg2")), "dg2")
        dq = Sem(st_.enter_context(nc.semaphore("dq")), "dq")

        ENGINE_SEMS = ("st", "sa", "sv")

        def wait(eng, sem, tgt):
            assert tgt <= sem.n, f"wait on future {sem.name}: {tgt} > {sem.n}"
            if tgt > 0:
                ops[eng].append(("wait", lambda e, h=sem.h, t=tgt: e.wait_ge(h, t), None, 0))

        def step(eng, sem, buildfn, k=1):
            sem.n += k
            if k > 0:
                kind = "inc" if sem.name in ENGINE_SEMS else "op"
                ops[eng].append((kind, lambda e, b=buildfn, h=sem.h, kk=k: b(e).then_inc(h, kk), sem.h, sem.n))
            else:
                ops[eng].append(("op", lambda e, b=buildfn: b(e), None, 0))
            return sem.n

        def op(eng, buildfn):
            ops[eng].append(("op", lambda e, b=buildfn: b(e), None, 0))

        def dma(q, sem, out_ap, in_ap):
            return step(q, sem, lambda e, o=out_ap, i=in_ap: e.dma_start(o, i), 16)

        def guard(eng, key):
            g = guards.get(key)
            if g is not None:
                wait(eng, g[0], g[1])

        own = {"tensor": st_t, "scalar": sa, "vector": sv}

        def rest(eng):
            s = own[eng]
            ops[eng].append(("wait", lambda e, h=s.h, t=s.n: e.wait_ge(h, t), None, 0))

        def free(key, sem, count):
            guards[key] = (sem, count)

        def emit_ln_rstd(src_a, src_b):
            guard("vector", "stats")
            step("vector", sv, lambda e: e.bn_stats(stats[:, 0:6], src_a))
            s2 = step("vector", sv, lambda e: e.bn_stats(stats[:, 6:12], src_b))
            free("stats", sv, s2)
            wait("vector", sv, s2)
            guard("vector", "mv")
            ag = step("vector", sv, lambda e: e.bn_aggr(mv[:], stats[:]))
            wait("vector", sv, ag)
            guard("vector", "veps")
            ve = step("vector", sv, lambda e: e.tensor_scalar(
                veps[:], mv[:, 1:2], EPS, None, op0=ALU.add))
            wait("scalar", sv, ve)
            guard("scalar", "rstd")
            l1 = step("scalar", sa, lambda e: e.activation(rstd[:], veps[:], AF.Ln))
            free("veps", sa, l1)
            wait("scalar", sa, l1)
            l2 = step("scalar", sa, lambda e: e.activation(rstd[:], rstd[:], AF.Exp, scale=-0.5))
            wait("vector", sa, l2)
            guard("vector", "negmr")
            nm = step("vector", sv, lambda e: e.tensor_scalar(
                negmr[:], mv[:, 0:1], rstd[:], -1.0, op0=ALU.mult, op1=ALU.mult))
            free("mv", sv, nm)
            return l2, nm

        dma("gpsimd", dc, ident_sb[:], ident_d)
        dma("gpsimd", dc, ind2_sb[:], ind2_d)
        const_ld = dc.n
        ones_set = step("vector", sv, lambda e: e.memset(ones1f[:], 1.0))

        p1 = contextlib.ExitStack()
        def sb1(name, shape, dt):
            return p1.enter_context(nc.sbuf_tensor(name, shape, dt))

        xa = sb1("xa", [128, 1024], F32)
        lnx_bf = sb1("lnx_bf", [128, 1024], BF16)
        lnxT_sb = sb1("lnxT_sb", [128, 4096], BF16)
        w1sb = sb1("w1sb", [128, 8 * 4096], BF16)
        w2sb = sb1("w2sb", [128, 32 * 1024], BF16)
        b1sb = sb1("b1sb", [128, 32], F32)
        b2sb = sb1("b2sb", [128, 8], F32)
        h_sb = sb1("h_sb", [128, 32 * 512], BF16)
        lt_sb2 = sb1("lt_sb2", [128, 2 * 8 * 512], BF16)
        ost = sb1("ost", [128, 8 * 512], BF16)
        sq_sb = sb1("sq_sb", [128, 2 * 512], BF16)

        for tc in range(TCN):
            for m4 in range(4):
                tt = tc * 4 + m4
                guard("sync", "xa")
                xld = dma("sync", dy, xa[:], x_d[tt * 128:(tt + 1) * 128, :])
                wait("vector", dy, xld)
                _, _ = emit_ln_rstd(xa[:, 0:512], xa[:, 512:1024])
                guard("vector", "lnx_bf")
                nrm = step("vector", sv, lambda e: e.tensor_scalar(
                    lnx_bf[:], xa[:], mv[:, 0:1], rstd[:],
                    op0=ALU.subtract, op1=ALU.mult))
                free("xa", sv, nrm)
                guards["mv"] = (sv, nrm)
                guards["rstd"] = (sv, nrm)
                wait("tensor", sv, nrm)
                if tt == 0:
                    wait("tensor", dc, const_ld)
                for c in range(NC):
                    slot, skey = (psTa, "psTa") if c % 2 == 0 else (psTb, "psTb")
                    guard("tensor", skey)
                    tr = step("tensor", st_t, lambda e, c=c, s=slot: e.transpose(
                        s[:, 0:128], lnx_bf[:, c * 128:(c + 1) * 128], ident_sb[:]))
                    if c >= 6:
                        free("lnx_bf", st_t, tr)
                    wait("vector", st_t, tr)
                    if m4 == 0 and c == 0:
                        guard("vector", "lnxT_sb")
                    cp = step("vector", sv, lambda e, c=c, m4=m4, s=slot: e.tensor_copy(
                        lnxT_sb[:, c * 512 + m4 * 128: c * 512 + m4 * 128 + 128],
                        s[:, 0:128]))
                    free(skey, sv, cp)
            wait("sync", sv, sv.n)
            for c in range(NC):
                dma("sync", dls, lnxT_s[c * 128:(c + 1) * 128, tc * 512:(tc + 1) * 512],
                    lnxT_sb[:, c * 512:(c + 1) * 512])
            free("lnxT_sb", dls, dls.n)

        if stage >= 1:
            wait("sync", dls, dls.n)

            def load_lt(gi, tc):
                par = gi % 2
                guard("sync", f"lt{par}")
                base = par * 8 * 512
                for c in range(NC):
                    dma("sync", dl, lt_sb2[:, base + c * 512: base + (c + 1) * 512],
                        lnxT_s[c * 128:(c + 1) * 128, tc * 512:(tc + 1) * 512])
                return dl.n

            gi_ld = 0
            gi_ld_counts = {}
            plan_chunks = [(p, tc) for p in ["k", "v", "q"][:p1_projs]
                           for tc in range(p1_tcs)]
            for pi, p in enumerate(["k", "v", "q"][:p1_projs]):
                guard("gpsimd", "w1sb")
                for kc in range(8):
                    dma("gpsimd", dg, w1sb[:, kc * 4096:(kc + 1) * 4096],
                        w1_d[p][kc * 128:(kc + 1) * 128, :])
                w1load = dg.n
                guard("gpsimd", "w2sb")
                for kc in range(32):
                    dma("gpsimd", dg2, w2sb[:, kc * 1024:(kc + 1) * 1024],
                        w2_d[p][kc * 128:(kc + 1) * 128, :])
                wload = dg2.n
                guard("sync", "b12")
                guard("sync", "b12_a")
                bl = dma("sync", db, b1sb[:], b1_d[p])
                bl = dma("sync", db, b2sb[:], b2_d[p])

                for tc in range(p1_tcs):
                    gi = pi * p1_tcs + tc
                    while gi_ld <= gi + 1 and gi_ld < len(plan_chunks):
                        gi_ld_counts[gi_ld] = load_lt(gi_ld, plan_chunks[gi_ld][1])
                        gi_ld += 1
                    par = gi % 2
                    lt_sb = lt_sb2[:, par * 8 * 512:(par + 1) * 8 * 512]
                    lt_ld = gi_ld_counts[min(gi + 1, len(plan_chunks) - 1)]
                    wait("tensor", dl, lt_ld)
                    if tc == 0:
                        wait("tensor", dg, w1load)
                    for mc in range(32):
                        hkey = "psH0" if mc % 2 == 0 else "psH1"
                        half = psA if mc % 2 == 0 else psB
                        guard("tensor", hkey)
                        for kc in range(8):
                            ra = lt_sb[:, kc * 512:(kc + 1) * 512]
                            step("tensor", st_t, lambda e, mc=mc, kc=kc, h=half, ra=ra: e.matmul(
                                h[:, 0:512],
                                w1sb[:, kc * 4096 + mc * 128: kc * 4096 + mc * 128 + 128],
                                ra,
                                start=(kc == 0), stop=(kc == 7)),
                                k=(1 if kc == 7 else 0))
                        mmd = st_t.n
                        wait("scalar", st_t, mmd)
                        if tc == 0 and mc == 0:
                            wait("scalar", db, bl)
                        guard("scalar", f"h_sb{mc}")
                        ge = step("scalar", sa, lambda e, mc=mc, h=half: e.activation(
                            h_sb[:, mc * 512:(mc + 1) * 512], h[:, 0:512], AF.Gelu,
                            bias=b1sb[:, mc:mc + 1]))
                        free(hkey, sa, ge)
                    pend_norm = []
                    gelu_done = sa.n
                    rest("scalar")
                    if tc == 0:
                        wait("tensor", dg2, wload)
                    wait("tensor", sa, gelu_done)
                    guards[f"lt{par}"] = (st_t, st_t.n)
                    if tc == p1_tcs - 1 and p1_parts != 9:
                        guards["w1sb"] = (st_t, st_t.n)
                    for oc in range(NC if p1_parts >= 2 else 0):
                        okey = "psO0" if oc % 2 == 0 else "psO1"
                        if p1_parts == 6:
                            half = psO if oc % 2 == 0 else psM
                            oslc = (0, 512)
                        else:
                            half = psA if oc % 2 == 0 else psB
                            oslc = (512, 1024)
                        guard("tensor", okey)
                        kcn = 8 if p1_parts in (7, 8, 9) else 32
                        for kc in range(kcn):
                            if p1_parts == 8:
                                rhs_ap = lt_sb[:, kc * 512:(kc + 1) * 512]
                            else:
                                rhs_ap = h_sb[:, kc * 512:(kc + 1) * 512]
                            if p1_parts == 9:
                                lhs_ap = w1sb[:, kc * 4096 + oc * 128: kc * 4096 + oc * 128 + 128]
                            else:
                                lhs_ap = w2sb[:, kc * 1024 + oc * 128: kc * 1024 + oc * 128 + 128]
                            step("tensor", st_t, lambda e, h=half, s=oslc, kcn=kcn,
                                 kc=kc, la=lhs_ap, ra=rhs_ap: e.matmul(
                                h[:, s[0]:s[1]], la, ra,
                                start=(kc == 0), stop=(kc == kcn - 1)),
                                k=(1 if kc == kcn - 1 else 0))
                        omm = st_t.n
                        wait("vector", st_t, omm)
                        if tc == 0 and oc == 0:
                            wait("vector", db, bl)
                        guard("vector", f"ost{oc}")
                        if p1_parts == 4:
                            ev = step("vector", sv, lambda e, oc=oc, h=half, s=oslc: e.tensor_copy(
                                ost[:, oc * 512:(oc + 1) * 512], h[:, s[0]:s[1]]))
                        else:
                            ev = step("vector", sv, lambda e, oc=oc, h=half, s=oslc: e.tensor_scalar(
                                ost[:, oc * 512:(oc + 1) * 512], h[:, s[0]:s[1]],
                                b2sb[:, oc:oc + 1], None, op0=ALU.add))
                        free(okey, sv, ev)
                        if p in ("k", "q") and p1_parts >= 3:
                            sqs = sq_sb[:, (oc % 2) * 512:(oc % 2) * 512 + 512]
                            guard("vector", f"sq{oc % 2}")
                            wait("vector", sv, ev)
                            sq = step("vector", sv, lambda e, s=sqs, oc=oc: e.tensor_tensor(
                                s, ost[:, oc * 512:(oc + 1) * 512],
                                ost[:, oc * 512:(oc + 1) * 512], op=ALU.mult))
                            pend_norm.append((oc, sq, sqs))
                        if pend_norm and (pend_norm[0][1] <= sv.n - 1 or oc == NC - 1):
                            pass
                        while pend_norm and (oc > pend_norm[0][0] or oc == NC - 1):
                            poc, psq, psqs = pend_norm.pop(0)
                            wait("tensor", sv, psq)
                            guard("tensor", "psM")
                            nmm = step("tensor", st_t, lambda e, s=psqs: e.matmul(
                                psM[0:2, 0:512], ind2_sb[:], s, start=True, stop=True))
                            free(f"sq{poc % 2}", st_t, nmm)
                            wait("vector", st_t, nmm)
                            buf = nmax_k if p == "k" else nmax_q
                            idx = poc * 8 + tc
                            rd = step("vector", sv, lambda e, b=buf, i=idx: e.tensor_reduce(
                                b[0:2, i:i + 1], psM[0:2, 0:512], axis=AXX, op=ALU.max))
                            free("psM", sv, rd)
                    for mc in range(32):
                        guards[f"h_sb{mc}"] = (st_t, st_t.n)
                    wait("sync", sv, sv.n)
                    if p1_parts != 5:
                        for oc in range(NC):
                            dma("sync", dls, pT_s[p][oc * 128:(oc + 1) * 128,
                                                     tc * 512:(tc + 1) * 512],
                                ost[:, oc * 512:(oc + 1) * 512])
                        for oc in range(NC):
                            guards[f"ost{oc}"] = (dls, dls.n)
                guards["w2sb"] = (st_t, st_t.n)
                if p1_parts == 9:
                    guards["w1sb"] = (st_t, st_t.n)
                guards["b12"] = (sv, sv.n)
                guards["b12_a"] = (sa, sa.n)

        if stage < 3 and debug_scratch:
            dbg_d = nc.dram_tensor("dbg", [128, 4096], BF16,
                                   kind="ExternalOutput").ap()
            wait("sync", sv, sv.n)
            wait("sync", sa, sa.n)
            wait("sync", st_t, st_t.n)
            dma("sync", dy, dbg_d, lnxT_sb[:] if stage < 1 else lt_sb2[:, 0:4096])

        p1_t = step("tensor", st_t, lambda e: e.nop())
        p1_a = step("scalar", sa, lambda e: e.nop())
        p1_v = step("vector", sv, lambda e: e.nop())
        p1_dy = dy.n
        p1_dls = dls.n
        p1.close()

        if stage >= 2:
            p2 = contextlib.ExitStack()
            def sb2(name, shape, dt):
                return p2.enter_context(nc.sbuf_tensor(name, shape, dt))

            OT = sb2("OT", [128, 8 * 4096], BF16)
            projw_sb = sb2("projw_sb", [128, 8 * 1024], BF16)
            projb_bc = sb2("projb_bc", [128, 1024], BF16)
            winrep_sb = sb2("winrep_sb", [128, 8 * 1024], BF16)
            w2bd_sb = sb2("w2bd_sb", [128, 8 * 128], BF16)
            wout_sb = sb2("wout_sb", [64, 1024], BF16)
            bout_bc = sb2("bout_bc", [128, 1024], BF16)
            w1f_sb = sb2("w1f_sb", [128, 8], F32)
            binr_sb = sb2("binr_sb", [128, 8], F32)
            b1f_sb = sb2("b1f_sb", [128, 8], F32)
            hb_sb = sb2("hb_sb", [128, 8], F32)
            fb2_sb = sb2("fb2_sb", [64, 1], F32)
            certj = sb2("certj", [128, 1], BF16)
            p2a = contextlib.ExitStack()
            def sba(name, shape, dt):
                return p2a.enter_context(nc.sbuf_tensor(name, shape, dt))
            kT_2 = sba("kT_2", [128, 2 * 4096], BF16)
            vT_2 = sba("vT_2", [128, 2 * 4096], BF16)
            qT_p = sba("qT_p", [128, 4096], BF16)
            vh = {(pp, e2): sba(f"vh{pp}{e2}", [128, 32 * 65 + 64], BF16)
                  for pp in range(2) for e2 in range(2)}
            qz = {0: sba("qz0", [128, 4096], BF16),
                  1: sba("qz1", [128, 4096], BF16)}
            expS = sba("expS", [128, 4 * 1024], BF16)
            o_2 = sba("o_2", [64, 3 * 512], BF16)
            recip_sb = sba("recip_sb", [1, 3 * 512], F32)
            sums_row = sba("sums_row", [1, 3 * 512], F32)
            stage_row = sums_row
            xt_sb = None

            wait("gpsimd", st_t, p1_t)
            wait("gpsimd", sa, p1_a)
            wait("gpsimd", sv, p1_v)
            wait("gpsimd", dy, p1_dy)
            wait("gpsimd", dls, p1_dls)
            for kc in range(8):
                dma("gpsimd", dg, projw_sb[:, kc * 1024:(kc + 1) * 1024],
                    projw_d[kc * 128:(kc + 1) * 128, :])
            for kc in range(8):
                dma("gpsimd", dg, winrep_sb[:, kc * 1024:(kc + 1) * 1024],
                    winrep_d[kc * 128:(kc + 1) * 128, :])
            for kc in range(8):
                dma("gpsimd", dg, w2bd_sb[:, kc * 128:(kc + 1) * 128],
                    w2bd_d[kc * 128:(kc + 1) * 128, :])
            dma("gpsimd", dg, wout_sb[:], wout_d)
            w3_ld = dg.n

            wait("sync", st_t, p1_t)
            wait("sync", sa, p1_a)
            wait("sync", sv, p1_v)
            wait("sync", dls, p1_dls)
            pbl = dma("sync", ds, stage_row[0:1, 0:1024], projb_d)
            dma("sync", ds, w1f_sb[:], w1flat_d)
            dma("sync", ds, binr_sb[:], binrep_d)
            dma("sync", ds, b1f_sb[:], b1flat_d)
            dma("sync", ds, fb2_sb[:], fb2_d)
            setup_ld = ds.n

            wait("vector", sv, p1_v)
            km = step("vector", sv, lambda e: e.tensor_reduce(
                kmax8[0:2, 0:8],
                nmax_k[0:2, :].rearrange("p (c t) -> p c t", t=8),
                axis=AXX, op=ALU.max))
            wait("vector", sv, km)
            q3 = nmax_q[0:2, :].rearrange("p (c t) -> p c t", t=8)
            k3 = kmax8[0:2, 0:8].rearrange("p (c t) -> p c t", t=1)
            q3b, k3b = bass.broadcast_tensor_aps(q3, k3)
            p3o = prodm[0:2, :].rearrange("p (c t) -> p c t", t=8)
            pr_ = step("vector", sv, lambda e: e.tensor_tensor(p3o, q3b, k3b, op=ALU.mult))
            wait("scalar", sv, pr_)
            wait("scalar", sa, p1_a)
            lp = step("scalar", sa, lambda e: e.activation(negm[0:2, :], prodm[0:2, :], AF.Ln))
            wait("scalar", sa, lp)
            exm = step("scalar", sa, lambda e: e.activation(
                negm[0:2, :], negm[0:2, :], AF.Exp, scale=0.5))
            wait("vector", sa, exm)
            ng = step("vector", sv, lambda e: e.tensor_scalar(
                negm[0:2, :], negm[0:2, :], -0.125, None, op0=ALU.mult))
            wait("gpsimd", sv, ng)
            nr = dma("gpsimd", dn, negm_row[0:1, 0:128], negm[0:2, 0:64])
            wait("tensor", dn, nr)
            wait("tensor", sv, ones_set)
            wait("tensor", sa, p1_a)
            wait("tensor", sv, p1_v)
            guard("tensor", "psM")
            mbm = step("tensor", st_t, lambda e: e.matmul(
                psM[0:128, 0:128], ones1f[0:1, :], negm_row[0:1, :],
                start=True, stop=True))
            wait("vector", st_t, mbm)
            mbc = step("vector", sv, lambda e: e.tensor_copy(mbias[:], psM[0:128, 0:128]))
            free("psM", sv, mbc)

            wait("tensor", ds, setup_ld)
            for dst in (projb_bc, bout_bc):
                for half in range(2):
                    guard("tensor", "psM")
                    r1 = step("tensor", st_t, lambda e, h=half: e.matmul(
                        psM[:, 0:512], ones1f[0:1, :],
                        stage_row[0:1, h * 512:(h + 1) * 512], start=True, stop=True))
                    wait("vector", st_t, r1)
                    cpb = step("vector", sv, lambda e, d=dst, h=half: e.tensor_copy(
                        d[:, h * 512:(h + 1) * 512], psM[:, 0:512]))
                    free("psM", sv, cpb)
                if dst is projb_bc:
                    wait("sync", st_t, st_t.n)
                    bol = dma("sync", db, stage_row[0:1, 0:1024], bout_d)
                    wait("tensor", db, bol)
                else:
                    guards["sums_0"] = (st_t, st_t.n)
                    guards["sums_1"] = (st_t, st_t.n)
            wait("vector", ds, setup_ld)
            hb1 = step("vector", sv, lambda e: e.tensor_tensor(
                hb_sb[:], w1f_sb[:], binr_sb[:], op=ALU.mult))
            wait("vector", sv, hb1)
            hb2 = step("vector", sv, lambda e: e.tensor_tensor(
                hb_sb[:], hb_sb[:], b1f_sb[:], op=ALU.add))
            for key in vh:
                step("vector", sv, lambda e, b=vh[key]: e.memset(b[:], 1.0))
            for e_ in range(2):
                step("vector", sv, lambda e, b=qz[e_]: e.memset(b[:], 0.0))
            vh_ones = sv.n

            def warm_burst(n=20):
                guard("tensor", "psM")
                for _ in range(n):
                    op("tensor", lambda e: e.matmul(
                        psM[64:128, 0:512], ident_sb[0:64, 0:64],
                        projb_bc[0:64, 0:512], start=True, stop=True))

            def load_kv(c):
                par = c % 2
                guard("sync", f"kv{par}")
                if c >= 2:
                    wait("sync", dp, kv_counts[c - 1])

                dma("sync", dp, kT_2[:, par * 4096:(par + 1) * 4096],
                    pT_s["k"][c * 128:(c + 1) * 128, :])
                dma("sync", dp, vT_2[:, par * 4096:(par + 1) * 4096],
                    pT_s["v"][c * 128:(c + 1) * 128, :])
                return dp.n

            wait("sync", dls, p1_dls)
            rc_counts = {}
            scp_counts = {}
            meta = {}
            kv_counts = {}
            kv_counts[0] = load_kv(0)
            qld = 0
            for c in range(NC):
                par = c % 2
                kT_p = kT_2[:, par * 4096:(par + 1) * 4096]
                vT_p = vT_2[:, par * 4096:(par + 1) * 4096]
                guard("sync", "pairbuf")
                if c >= 1:
                    wait("sync", dq, qld)
                qld = dma("sync", dq, qT_p[:], pT_s["q"][c * 128:(c + 1) * 128, :])
                wait("tensor", dp, kv_counts[c])
                wait("tensor", dq, qld)
                if c == 0:
                    wait("tensor", sv, vh_ones)
                def emit_vtr(cc, e2, kc, first_guard=False):
                    slot, skey = (psTa, "psTa") if e2 == 0 else (psTb, "psTb")
                    b2 = e2 * 64
                    vTcc = vT_2[:, (cc % 2) * 4096:((cc % 2) + 1) * 4096]
                    guard("tensor", skey)
                    va = vTcc[b2:b2 + 64, kc * 128:(kc + 1) * 128]
                    ia = ident_sb[b2:b2 + 64, b2:b2 + 64]
                    tr = step("tensor", st_t, lambda e, s=slot, va=va, ia=ia: e.transpose(
                        s[:, 0:64], va, ia))
                    wait("vector", st_t, tr)
                    if first_guard:
                        guard("vector", f"vhfree{cc % 2}")
                    cp = step("vector", sv, lambda e, k2=(cc % 2, e2), kc=kc,
                              s=slot: e.tensor_copy(
                        vh[k2][:, kc * 65: kc * 65 + 64], s[:, 0:64]))
                    free(skey, sv, cp)
                    return tr

                tr_first = None
                if c == 0:
                    for e_ in range(2):
                        for kc in range(32):
                            tr = emit_vtr(0, e_, kc, first_guard=(e_ == 0 and kc == 0))
                            if tr_first is None:
                                tr_first = tr
                pend_vtr = [(c + 1, e2, kc) for e2 in range(2)
                            for kc in range(32)] if c + 1 < NC else []
                wait("vector", dq, qld)
                guard("vector", "qz")
                step("vector", sv, lambda e: e.tensor_copy(
                    qz[0][0:64, :], qT_p[0:64, :]))
                qzdone = step("vector", sv, lambda e: e.tensor_copy(
                    qz[1][64:128, :], qT_p[64:128, :]))
                vtr_done = sv.n
                if c + 1 < NC:
                    wait("sync", st_t, st_t.n if tr_first is None else tr_first)
                    kv_counts[c + 1] = load_kv(c + 1)
                wait("tensor", sv, vtr_done)
                wait("scalar", ds, setup_ld)
                wait("scalar", sv, mbc)

                for qc in range(TCN):
                    for e_ in range(2):
                        base = e_ * 64
                        midx = e_ * 64 + c * 8 + qc
                        it = (c * TCN + qc) * 2 + e_
                        crt_counts = {}
                        for kk in range(16):
                            sbank, bkey = (psA, "psSA") if kk % 2 == 0 else (psB, "psSB")
                            eslot = kk % 4
                            guard("tensor", bkey)
                            for j in range(2):
                                kc = kk * 2 + j
                                ka = kT_p[:, kc * 128:(kc + 1) * 128]
                                qa = qz[e_][:, qc * 512:(qc + 1) * 512]
                                step("tensor", st_t, lambda e, j=j, s=sbank,
                                     ka=ka, qa=qa: e.matmul(
                                    s[:, j * 512:(j + 1) * 512], ka, qa,
                                    start=True, stop=True), k=(1 if j == 1 else 0))
                            smm = st_t.n
                            if pend_vtr and qc >= 6:
                                if pend_vtr[0][1] == 0 and pend_vtr[0][2] == 0:
                                    wait("tensor", dp, kv_counts[c + 1])
                                cc2, e2, kc2 = pend_vtr.pop(0)
                                emit_vtr(cc2, e2, kc2,
                                         first_guard=(e2 == 0 and kc2 == 0))
                            if kk == 3 and it > 1:
                                par_p = (it - 2) % 3
                                guard("tensor", "psM")
                                wait("tensor", sv, rc_counts[par_p])
                                bcm = step("tensor", st_t, lambda e, pp=par_p: e.matmul(
                                    psM[0:64, 0:512], ones1f[0:1, 0:64],
                                    recip_sb[0:1, pp * 512:(pp + 1) * 512],
                                    start=True, stop=True))
                                free(f"rc_{par_p}", st_t, bcm)
                                bcm_prev = bcm
                            if kk > 0:
                                wait("tensor", sv, crt_counts[kk - 1])
                                if kk == 1:
                                    guard("tensor", "psO_attn")
                                for j in range(2):
                                    kc = (kk - 1) * 2 + j
                                    es = (kk - 1) % 4
                                    step("tensor", st_t, lambda e, kc=kc, j=j, es=es,
                                         k2=(c % 2, e_): e.matmul(
                                        psO[0:128, 0:512],
                                        vh[k2][:, kc * 65: kc * 65 + 128],
                                        expS[:, es * 1024 + j * 512:
                                             es * 1024 + (j + 1) * 512],
                                        start=(kc == 0), stop=(kc == 31)),
                                        k=(1 if j == 1 else 0))
                                free(f"expS{(kk - 1) % 4}", st_t, st_t.n)
                            wait("scalar", st_t, smm)
                            guard("scalar", f"expS{eslot}")
                            exq = step("scalar", sa, lambda e, s=sbank, es=eslot,
                                       m=midx: e.activation(
                                expS[:, es * 1024:(es + 1) * 1024], s[:, 0:1024],
                                AF.Exp, bias=mbias[:, m:m + 1], scale=0.125))
                            free(bkey, sa, exq)
                            wait("vector", sa, exq)
                            guard("vector", "certj")
                            crt_counts[kk] = step("vector", sv, lambda e, es=eslot: e.tensor_copy(
                                certj[:], expS[:, es * 1024 + 1023: es * 1024 + 1024]))
                            free("certj", sv, crt_counts[kk])
                            if kk == 6 and it > 0:
                                pr = (it - 1) % 3
                                wait("vector", sv, scp_counts[pr])
                                guard("vector", "rcscr")
                                guard("vector", f"rc_{pr}")
                                rc_counts[pr] = step("vector", sv, lambda e, pr=pr: e.reciprocal(
                                    recip_sb[0:1, pr * 512:(pr + 1) * 512],
                                    sums_row[0:1, pr * 512:(pr + 1) * 512]))
                                free("rcscr", sv, rc_counts[pr])
                                free(f"sums_{pr}", sv, rc_counts[pr])
                            if kk == 9 and it > 1:
                                par_p = (it - 2) % 3
                                wait("vector", st_t, bcm_prev)
                                pm = meta[(it - 2) % 3]
                                mul = step("vector", sv, lambda e, pm=pm, pp=par_p: e.tensor_tensor(
                                    OT[pm[0]:pm[0] + 64,
                                       pm[1] * 4096 + pm[2] * 512: pm[1] * 4096 + (pm[2] + 1) * 512],
                                    o_2[:, pp * 512:(pp + 1) * 512],
                                    psM[0:64, 0:512], op=ALU.mult))
                                free("psM", sv, mul)
                                free(f"o2_{par_p}", sv, mul)
                        wait("tensor", sv, crt_counts[15])
                        for j in range(2):
                            kc = 15 * 2 + j
                            step("tensor", st_t, lambda e, kc=kc, j=j, k2=(c % 2, e_): e.matmul(
                                psO[0:128, 0:512],
                                vh[k2][:, kc * 65: kc * 65 + 128],
                                expS[:, 3 * 1024 + j * 512: 3 * 1024 + (j + 1) * 512],
                                start=False, stop=(j == 1)), k=(1 if j == 1 else 0))
                        free("expS3", st_t, st_t.n)
                        pv_done = st_t.n
                        par = it % 3
                        wait("vector", st_t, pv_done)
                        guard("vector", f"o2_{par}")
                        ocp = step("vector", sv, lambda e, par=par: e.tensor_copy(
                            o_2[:, par * 512:(par + 1) * 512], psO[0:64, 0:512]))
                        guard("vector", f"sums_{par}")
                        scp = step("vector", sv, lambda e, par=par: e.tensor_copy(
                            sums_row[0:1, par * 512:(par + 1) * 512], psO[64:65, 0:512]))
                        free("psO_attn", sv, scp)
                        scp_counts[par] = scp
                        meta[it % 3] = (base, c, qc)
                assert not pend_vtr
                guards["pairbuf"] = (st_t, st_t.n)
                guards[f"vhfree{par}"] = (st_t, st_t.n)
                guards[f"kv{par}"] = (st_t, st_t.n)
                guards["qz"] = (st_t, st_t.n)

            n_it = NC * TCN * 2
            prf = (n_it - 1) % 3
            wait("vector", sv, scp_counts[prf])
            guard("vector", "rcscr")
            rc_counts[prf] = step("vector", sv, lambda e, pr=prf: e.reciprocal(
                recip_sb[0:1, pr * 512:(pr + 1) * 512],
                sums_row[0:1, pr * 512:(pr + 1) * 512]))
            free("rcscr", sv, rc_counts[prf])
            for itf in (n_it - 2, n_it - 1):
                par_p = itf % 3
                guard("tensor", "psM")
                wait("tensor", sv, rc_counts[par_p])
                bcm = step("tensor", st_t, lambda e, pp=par_p: e.matmul(
                    psM[0:64, 0:512], ones1f[0:1, 0:64],
                    recip_sb[0:1, pp * 512:(pp + 1) * 512], start=True, stop=True))
                wait("vector", st_t, bcm)
                pm = meta[itf % 3]
                mul = step("vector", sv, lambda e, pm=pm, pp=par_p: e.tensor_tensor(
                    OT[pm[0]:pm[0] + 64,
                       pm[1] * 4096 + pm[2] * 512: pm[1] * 4096 + (pm[2] + 1) * 512],
                    o_2[:, pp * 512:(pp + 1) * 512],
                    psM[0:64, 0:512], op=ALU.mult))
                free("psM", sv, mul)
                free(f"o2_{par_p}", sv, mul)

            attn_v = sv.n
            attn_t = st_t.n
            attn_a = sa.n
            attn_dp = dp.n
            p2a.close()

            xt_sb = sb2("xt_sb", [128, 1024], F32)
            attn_sb = sb2("attn_sb", [128, 1024], F32)
            x2_sb = sb2("x2_sb", [128, 8 * 1024], F32)
            ln3_bf = sb2("ln3_bf", [128, 1024], BF16)
            ln3T_sb = sb2("ln3T_sb", [128, 2 * 8 * 512], BF16)
            onesbf = sb2("onesbf", [1, 128], BF16)
            hf_sb = sb2("hf_sb", [128, 8 * 512], BF16)
            g_sb = sb2("g_sb", [64, 512], BF16)
            f_sb = sb2("f_sb", [128, 1024], F32)
            out_sb = sb2("out_sb", [128, 1024], F32)

            if stage >= 3:
                wait("tensor", sv, attn_v)
                wait("tensor", sa, attn_a)
                wait("tensor", dg, w3_ld)
                wait("scalar", st_t, attn_t)
                wait("scalar", sv, attn_v)
                wait("sync", st_t, attn_t)
                wait("sync", sv, attn_v)
                wait("sync", sa, attn_a)
                wait("sync", dp, attn_dp)
                out_count = 0
                pending = None
                prj = {}
                ob = step("vector", sv, lambda e: e.memset(onesbf[:], 1.0))
                wait("tensor", sv, ob)

                def proj_mms(t):
                    guard("tensor", "psA3")
                    for nn2 in range(2):
                        for kc in range(8):
                            step("tensor", st_t, lambda e, nn2=nn2, kc=kc, t=t: e.matmul(
                                psA[:, nn2 * 512:(nn2 + 1) * 512],
                                OT[:, kc * 4096 + t * 128: kc * 4096 + (t + 1) * 128],
                                projw_sb[:, kc * 1024 + nn2 * 512:
                                         kc * 1024 + (nn2 + 1) * 512],
                                start=(kc == 0), stop=False), k=0)
                        step("tensor", st_t, lambda e, nn2=nn2: e.matmul(
                            psA[:, nn2 * 512:(nn2 + 1) * 512],
                            onesbf[0:1, 0:128],
                            projb_bc[0:1, nn2 * 512:(nn2 + 1) * 512],
                            start=False, stop=True), k=(1 if nn2 == 1 else 0))
                    prj[t] = st_t.n

                def srep_pair(k, mis):
                    par = k % 2
                    if mis[0] == 0:
                        wait("tensor", sv, pending[1])
                    for mi in mis:
                        hh = mi % 2
                        guard("tensor", f"psF{hh}")
                        for kc in range(8):
                            step("tensor", st_t, lambda e, mi=mi, kc=kc, hh=hh, par=par: e.matmul(
                                psB[:, hh * 512:(hh + 1) * 512],
                                winrep_sb[:, kc * 1024 + mi * 128:
                                          kc * 1024 + mi * 128 + 128],
                                ln3T_sb[:, par * 4096 + kc * 512:
                                        par * 4096 + (kc + 1) * 512],
                                start=(kc == 0), stop=(kc == 7)),
                                k=(1 if kc == 7 else 0))
                    if mis[-1] == 7:
                        guards[f"ln3Tw{par}"] = (st_t, st_t.n)

                def srep_relu(k, mis):
                    if k == 0 and mis[0] == 0:
                        wait("scalar", ds, setup_ld)
                        wait("scalar", sv, hb2)
                    for mi in mis:
                        hh = mi % 2
                        wait("scalar", st_t, srep_done[mi])
                        guard("scalar", f"hf{mi}")
                        rl = step("scalar", sa, lambda e, mi=mi, hh=hh: e.activation(
                            hf_sb[:, mi * 512:(mi + 1) * 512],
                            psB[:, hh * 512:(hh + 1) * 512], AF.Relu,
                            scale=w1f_sb[:, mi:mi + 1], bias=hb_sb[:, mi:mi + 1]))
                        free(f"psF{hh}", sa, rl)
                    if mis[-1] == 7:
                        guards["relu_done"] = (sa, sa.n)

                def g_mms():
                    g = guards["relu_done"]
                    wait("tensor", g[0], g[1])
                    guard("tensor", "psOM4")
                    guard("tensor", "psO_g")
                    for mi in range(8):
                        step("tensor", st_t, lambda e, mi=mi: e.matmul(
                            psO[:, 0:512], w2bd_sb[:, mi * 128:(mi + 1) * 128],
                            hf_sb[:, mi * 512:(mi + 1) * 512],
                            start=(mi == 0), stop=(mi == 7)),
                            k=(1 if mi == 7 else 0))
                    gmm = st_t.n
                    for mi in range(8):
                        guards[f"hf{mi}"] = (st_t, gmm)
                    wait("scalar", st_t, gmm)
                    guard("scalar", "g_sb")
                    gv = step("scalar", sa, lambda e: e.activation(
                        g_sb[:], psO[0:64, 0:512], AF.Identity, bias=fb2_sb[:]))
                    free("psO_g", sa, gv)

                def g_cert():
                    wait("vector", sa, sa.n)
                    guard("vector", "certj")
                    gcrt = step("vector", sv, lambda e: e.tensor_copy(
                        certj[0:64, 0:1], g_sb[:, 511:512]))
                    free("certj", sv, gcrt)
                    guards["gcrt"] = (sv, gcrt)

                def wout_chain(k, j2):
                    nonlocal out_count
                    tt = k * 4 + j2
                    gc = guards["gcrt"]
                    wait("tensor", gc[0], gc[1])
                    guard("tensor", "psOM4")
                    for nn2 in range(2):
                        half = psO if nn2 == 0 else psM
                        step("tensor", st_t, lambda e, j2=j2, nn2=nn2, h=half: e.matmul(
                            h[:, 0:512],
                            g_sb[:, j2 * 128:(j2 + 1) * 128],
                            wout_sb[0:64, nn2 * 512:(nn2 + 1) * 512],
                            start=True, stop=False), k=0)
                        step("tensor", st_t, lambda e, nn2=nn2, h=half: e.matmul(
                            h[:, 0:512],
                            onesbf[0:1, 0:128],
                            bout_bc[0:1, nn2 * 512:(nn2 + 1) * 512],
                            start=False, stop=True), k=(1 if nn2 == 1 else 0))
                    fmm = st_t.n
                    if j2 == 3:
                        guards["g_sb"] = (st_t, fmm)
                    wait("scalar", st_t, fmm)
                    guard("scalar", "f_sb")
                    for nn2 in range(2):
                        half = psO if nn2 == 0 else psM
                        fs = step("scalar", sa, lambda e, nn2=nn2, h=half: e.activation(
                            f_sb[:, nn2 * 512:(nn2 + 1) * 512],
                            h[:, 0:512], AF.Identity))
                    free("psOM4", sa, fs)
                    wait("vector", sa, fs)
                    emit_ln_rstd(f_sb[:, 0:512], f_sb[:, 512:1024])
                    nm4 = sv.n
                    wait("scalar", sv, nm4)
                    guard("scalar", "attn_sb")
                    t4 = step("scalar", sa, lambda e: e.activation(
                        attn_sb[:], f_sb[:], AF.Identity,
                        scale=rstd[:, 0:1], bias=negmr[:, 0:1]))
                    free("f_sb", sa, t4)
                    guards["negmr"] = (sa, t4)
                    guards["rstd"] = (sa, t4)
                    wait("vector", sa, t4)
                    guard("vector", "out_sb")
                    slot = tt % 8
                    fin = step("vector", sv, lambda e, slot=slot: e.tensor_tensor(
                        out_sb[:], attn_sb[:],
                        x2_sb[:, slot * 1024:(slot + 1) * 1024], op=ALU.add))
                    guards["attn_sb"] = (sv, fin)
                    guards[f"x2_{slot}"] = (sv, fin)
                    wait("sync", sv, fin)
                    dma("sync", do, out_d[tt * 128:(tt + 1) * 128, :], out_sb[:])
                    out_count += 1
                    guards["out_sb"] = (do, out_count * 16)

                proj_mms(0)
                srep_done = {}
                for t in range(32):
                    c, j = divmod(t, 4)
                    ffn = pending is not None
                    guard("sync", "xt_sb")
                    xld = dma("sync", dy, xt_sb[:], x_d[t * 128:(t + 1) * 128, :])
                    wait("scalar", st_t, prj[t])
                    guard("scalar", "attn_sb")
                    for nn2 in range(2):
                        pa = step("scalar", sa, lambda e, nn2=nn2: e.activation(
                            attn_sb[:, nn2 * 512:(nn2 + 1) * 512],
                            psA[:, nn2 * 512:(nn2 + 1) * 512], AF.Identity))
                    free("psA3", sa, pa)
                    if t + 1 < 32:
                        proj_mms(t + 1)
                    wait("vector", sa, pa)
                    emit_ln_rstd(attn_sb[:, 0:512], attn_sb[:, 512:1024])
                    nm2 = sv.n
                    wait("scalar", sv, nm2)
                    guard("scalar", "f_sb")
                    t2 = step("scalar", sa, lambda e: e.activation(
                        f_sb[:], attn_sb[:], AF.Identity,
                        scale=rstd[:, 0:1], bias=negmr[:, 0:1]))
                    guards["negmr"] = (sa, t2)
                    guards["rstd"] = (sa, t2)
                    guards["attn_sb"] = (sa, t2)
                    if ffn:
                        if j == 0:
                            srep_pair(pending[0], (0, 1))
                            srep_done[0] = srep_done[1] = st_t.n
                            srep_relu(pending[0], (0, 1))
                        elif j == 1:
                            srep_pair(pending[0], (4, 5))
                            srep_done[4] = srep_done[5] = st_t.n
                            srep_relu(pending[0], (4, 5))
                        elif j == 2:
                            g_mms()
                    wait("vector", sa, t2)
                    wait("vector", dy, xld)
                    guard("vector", f"x2_{t % 8}")
                    x2w = step("vector", sv, lambda e, s=t % 8: e.tensor_tensor(
                        x2_sb[:, s * 1024:(s + 1) * 1024], f_sb[:],
                        xt_sb[:], op=ALU.add))
                    free("xt_sb", sv, x2w)
                    free("f_sb", sv, x2w)
                    wait("vector", sv, x2w)
                    _, nm3 = emit_ln_rstd(x2_sb[:, (t % 8) * 1024: (t % 8) * 1024 + 512],
                                 x2_sb[:, (t % 8) * 1024 + 512: (t % 8) * 1024 + 1024])
                    wait("vector", sv, nm3)
                    guard("vector", "ln3_bf")
                    n3 = step("vector", sv, lambda e, s=t % 8: e.tensor_scalar(
                        ln3_bf[:], x2_sb[:, s * 1024:(s + 1) * 1024],
                        rstd[:, 0:1], negmr[:, 0:1], op0=ALU.mult, op1=ALU.add))
                    guards["negmr"] = (sv, n3)
                    guards["rstd"] = (sv, n3)
                    wait("tensor", sv, n3)
                    for c2 in range(NC):
                        slot2, skey = (psTa, "psTa") if c2 % 2 == 0 else (psTb, "psTb")
                        guard("tensor", skey)
                        tr3 = step("tensor", st_t, lambda e, c2=c2, s=slot2: e.transpose(
                            s[:, 128:256], ln3_bf[:, c2 * 128:(c2 + 1) * 128],
                            ident_sb[:]))
                        if c2 >= 6:
                            free("ln3_bf", st_t, tr3)
                        wait("vector", st_t, tr3)
                        if j == 0 and c2 == 0:
                            guard("vector", f"ln3Tw{c % 2}")
                        cp3 = step("vector", sv, lambda e, c2=c2, j=j, s=slot2, par=c % 2: e.tensor_copy(
                            ln3T_sb[:, par * 4096 + c2 * 512 + j * 128:
                                    par * 4096 + c2 * 512 + j * 128 + 128],
                            s[:, 128:256]))
                        free(skey, sv, cp3)
                    if ffn:
                        if j == 0:
                            srep_pair(pending[0], (2, 3))
                            srep_done[2] = srep_done[3] = st_t.n
                            srep_relu(pending[0], (2, 3))
                        elif j == 1:
                            srep_pair(pending[0], (6, 7))
                            srep_done[6] = srep_done[7] = st_t.n
                            srep_relu(pending[0], (6, 7))
                        elif j == 2:
                            g_cert()
                            wout_chain(pending[0], 0)
                            wout_chain(pending[0], 1)
                        elif j == 3:
                            wout_chain(pending[0], 2)
                            wout_chain(pending[0], 3)
                    if j == 3:
                        pending = (c, sv.n)
                for pr_ in ((0, 1), (2, 3), (4, 5), (6, 7)):
                    srep_pair(pending[0], pr_)
                    for mi in pr_:
                        srep_done[mi] = st_t.n
                    srep_relu(pending[0], pr_)
                g_mms()
                g_cert()
                for j2 in range(4):
                    wout_chain(pending[0], j2)
                wait("sync", do, out_count * 16)
            p2.close()

        if stage < 3:
            wait("sync", st_t, st_t.n)
            wait("sync", sa, sa.n)
            wait("sync", sv, sv.n)
            for s_ in (dy, dg, dg2, dc, dn, db, dl, dls, dp, dq, ds):
                wait("sync", s_, s_.n)

        block = st_.enter_context(nc.Block())

        def replay(eng, lst):
            n = len(lst)
            for i, (kind, f, semh, cnt) in enumerate(lst):
                f(eng)
                if kind == "inc" and (i + 1 >= n or lst[i + 1][0] != "wait"):
                    eng.wait_ge(semh, cnt)

        @block.tensor
        def _(eng):
            replay(eng, ops["tensor"])

        @block.scalar
        def _(eng):
            replay(eng, ops["scalar"])

        @block.vector
        def _(eng):
            replay(eng, ops["vector"])

        @block.sync
        def _(eng):
            replay(eng, ops["sync"])

        @block.gpsimd
        def _(eng):
            replay(eng, ops["gpsimd"])

    return nc


def _host_common(kw):
    f = lambda a: np.ascontiguousarray(np.asarray(a, dtype=np.float32))
    pr = lambda b, k: np.ascontiguousarray(f(b).reshape(k, 128).T)
    common = {
        "qw1": f(kw["L0_attn_q_w1"]), "qw2": f(kw["L0_attn_q_w2"]),
        "kw1": f(kw["L0_attn_k_w1"]), "kw2": f(kw["L0_attn_k_w2"]),
        "vw1": f(kw["L0_attn_v_w1"]), "vw2": f(kw["L0_attn_v_w2"]),
        "qb1r": pr(kw["L0_attn_q_b1"], 32), "qb2r": pr(kw["L0_attn_q_b2"], 8),
        "kb1r": pr(kw["L0_attn_k_b1"], 32), "kb2r": pr(kw["L0_attn_k_b2"], 8),
        "vb1r": pr(kw["L0_attn_v_b1"], 32), "vb2r": pr(kw["L0_attn_v_b2"], 8),
        "projw": f(kw["L0_attn_proj_w"]),
        "projb": f(kw["L0_attn_proj_b"]).reshape(1, 1024),
        "winrep": np.ascontiguousarray(np.repeat(f(kw["L0_ffn_Win"]), 16, axis=1)),
        "w1flat": pr(np.asarray(kw["L0_ffn_W1"]).reshape(1024), 8),
        "binrep": pr(np.repeat(f(kw["L0_ffn_bin_"]), 16), 8),
        "b1flat": pr(np.asarray(kw["L0_ffn_b1"]).reshape(1024), 8),
        "fb2": f(kw["L0_ffn_b2"]).reshape(64, 1),
        "wout": f(kw["L0_ffn_Wout"]),
        "bout": f(kw["L0_ffn_bout"]).reshape(1, 1024),
        "ident": np.eye(128, dtype=np.float32).astype(ml_dtypes.bfloat16),
    }
    w2bd = np.zeros((1024, 128), np.float32)
    w2bd[np.arange(1024), np.arange(1024) // 16] = f(kw["L0_ffn_W2"]).reshape(1024)
    common["w2bd"] = w2bd
    ind2 = np.zeros((128, 2), np.float32)
    ind2[0:64, 0] = 1.0
    ind2[64:128, 1] = 1.0
    common["ind2"] = ind2.astype(ml_dtypes.bfloat16)
    return common


def kernel(L0_attn_x, **kw):
    X = np.ascontiguousarray(np.asarray(L0_attn_x, dtype=np.float32))
    common = _host_common(kw)
    nc = build()
    in_maps = [dict(common, x=np.ascontiguousarray(X[b])) for b in range(N_CORES)]
    res = run_bass_kernel_spmd(nc, in_maps, core_ids=list(range(N_CORES)))
    return np.stack([np.asarray(res.results[b]["out"]) for b in range(N_CORES)], axis=0)


# version 16
# speedup vs baseline: 1.2159x; 1.2085x over previous
import contextlib
import numpy as np
import ml_dtypes
import concourse.bass as bass
import concourse.mybir as mybir
from concourse.bass_utils import run_bass_kernel_spmd

F32 = mybir.dt.float32
BF16 = mybir.dt.bfloat16
AF = mybir.ActivationFunctionType
ALU = mybir.AluOpType
AXX = mybir.AxisListType.X

N_CORES = 8
T = 4096
D = 1024
NC = 8
TCN = 8
EPS = 1e-5
LN8TH = float(np.log(0.125))


class Sem:
    def __init__(self, handle, name):
        self.h = handle
        self.n = 0
        self.name = name


def build(stage=3, debug_scratch=False, p1_projs=3, p1_tcs=TCN, p1_parts=3):
    nc = bass.Bass("TRN2", target_bir_lowering=False, debug=False)
    ops = {"tensor": [], "scalar": [], "vector": [], "sync": [], "gpsimd": []}
    guards = {}

    def din(name, shape, dt=F32):
        return nc.dram_tensor(name, shape, dt, kind="ExternalInput").ap()

    x_d = din("x", [T, D])
    w1_d = {p: din(f"{p}w1", [D, 4096]) for p in "qkv"}
    w2_d = {p: din(f"{p}w2", [4096, D]) for p in "qkv"}
    b1_d = {p: din(f"{p}b1r", [128, 32]) for p in "qkv"}
    b2_d = {p: din(f"{p}b2r", [128, 8]) for p in "qkv"}
    projw_d = din("projw", [D, D])
    projb_d = din("projb", [1, D])
    winrep_d = din("winrep", [D, 1024])
    w1flat_d = din("w1flat", [128, 8])
    binrep_d = din("binrep", [128, 8])
    b1flat_d = din("b1flat", [128, 8])
    w2bd_d = din("w2bd", [1024, 128])
    fb2_d = din("fb2", [64, 1])
    wout_d = din("wout", [64, D])
    bout_d = din("bout", [1, D])
    ident_d = din("ident", [128, 128], BF16)
    ind2_d = din("ind2", [128, 2], BF16)

    out_d = nc.dram_tensor("out", [T, D], F32, kind="ExternalOutput").ap()
    skind = "ExternalOutput" if debug_scratch else "Internal"
    lnxT_s = nc.dram_tensor("lnxT_s", [D, T], BF16, kind=skind).ap()
    pT_s = {p: nc.dram_tensor(f"{p}T_s", [D, T], BF16, kind=skind).ap() for p in "qkv"}

    st_ = contextlib.ExitStack()
    with st_:
        def sbp(name, shape, dt):
            return st_.enter_context(nc.sbuf_tensor(name, shape, dt))

        ident_sb = sbp("ident_sb", [128, 128], BF16)
        ind2_sb = sbp("ind2_sb", [128, 2], BF16)
        ones1f = sbp("ones1f", [1, 128], F32)
        nmax_q = sbp("nmax_q", [2, 64], F32)
        nmax_k = sbp("nmax_k", [2, 64], F32)
        kmax8 = sbp("kmax8", [2, 8], F32)
        prodm = sbp("prodm", [2, 64], F32)
        negm = sbp("negm", [2, 64], F32)
        negm_row = sbp("negm_row", [1, 128], F32)
        mbias = sbp("mbias", [128, 128], F32)
        stats = sbp("stats", [128, 12], F32)
        mv = sbp("mv", [128, 2], F32)
        veps = sbp("veps", [128, 1], F32)
        rstd = sbp("rstd", [128, 1], F32)
        negmr = sbp("negmr", [128, 1], F32)

        psA = st_.enter_context(nc.psum_tensor("psA", [128, 1024], F32))
        psB = st_.enter_context(nc.psum_tensor("psB", [128, 1024], F32))
        psO = st_.enter_context(nc.psum_tensor("psO", [128, 512], F32))
        psM = st_.enter_context(nc.psum_tensor("psM", [128, 512], F32))
        psTa = st_.enter_context(nc.psum_tensor("psTa", [128, 256], BF16))
        psTb = st_.enter_context(nc.psum_tensor("psTb", [128, 256], BF16))

        st_t = Sem(st_.enter_context(nc.semaphore("st")), "st")
        sa = Sem(st_.enter_context(nc.semaphore("sa")), "sa")
        sv = Sem(st_.enter_context(nc.semaphore("sv")), "sv")
        dy = Sem(st_.enter_context(nc.semaphore("dy")), "dy")
        dg = Sem(st_.enter_context(nc.semaphore("dg")), "dg")
        do = Sem(st_.enter_context(nc.semaphore("do")), "do")
        dc = Sem(st_.enter_context(nc.semaphore("dc")), "dc")
        dn = Sem(st_.enter_context(nc.semaphore("dn")), "dn")
        db = Sem(st_.enter_context(nc.semaphore("db")), "db")
        dl = Sem(st_.enter_context(nc.semaphore("dl")), "dl")
        dls = Sem(st_.enter_context(nc.semaphore("dls")), "dls")
        dp = Sem(st_.enter_context(nc.semaphore("dp")), "dp")
        ds = Sem(st_.enter_context(nc.semaphore("ds")), "ds")
        dg2 = Sem(st_.enter_context(nc.semaphore("dg2")), "dg2")
        dq = Sem(st_.enter_context(nc.semaphore("dq")), "dq")

        ENGINE_SEMS = ("st", "sa", "sv")

        def wait(eng, sem, tgt):
            assert tgt <= sem.n, f"wait on future {sem.name}: {tgt} > {sem.n}"
            if tgt > 0:
                ops[eng].append(("wait", lambda e, h=sem.h, t=tgt: e.wait_ge(h, t), None, 0))

        def step(eng, sem, buildfn, k=1):
            sem.n += k
            if k > 0:
                kind = "inc" if sem.name in ENGINE_SEMS else "op"
                ops[eng].append((kind, lambda e, b=buildfn, h=sem.h, kk=k: b(e).then_inc(h, kk), sem.h, sem.n))
            else:
                ops[eng].append(("op", lambda e, b=buildfn: b(e), None, 0))
            return sem.n

        def op(eng, buildfn):
            ops[eng].append(("op", lambda e, b=buildfn: b(e), None, 0))

        def dma(q, sem, out_ap, in_ap):
            return step(q, sem, lambda e, o=out_ap, i=in_ap: e.dma_start(o, i), 16)

        def guard(eng, key):
            g = guards.get(key)
            if g is not None:
                wait(eng, g[0], g[1])

        own = {"tensor": st_t, "scalar": sa, "vector": sv}

        def rest(eng):
            s = own[eng]
            ops[eng].append(("wait", lambda e, h=s.h, t=s.n: e.wait_ge(h, t), None, 0))

        def free(key, sem, count):
            guards[key] = (sem, count)

        def emit_ln_rstd(src_a, src_b):
            guard("vector", "stats")
            step("vector", sv, lambda e: e.bn_stats(stats[:, 0:6], src_a))
            s2 = step("vector", sv, lambda e: e.bn_stats(stats[:, 6:12], src_b))
            free("stats", sv, s2)
            wait("vector", sv, s2)
            guard("vector", "mv")
            ag = step("vector", sv, lambda e: e.bn_aggr(mv[:], stats[:]))
            wait("vector", sv, ag)
            guard("vector", "veps")
            ve = step("vector", sv, lambda e: e.tensor_scalar(
                veps[:], mv[:, 1:2], EPS, None, op0=ALU.add))
            wait("scalar", sv, ve)
            guard("scalar", "rstd")
            l1 = step("scalar", sa, lambda e: e.activation(rstd[:], veps[:], AF.Ln))
            free("veps", sa, l1)
            wait("scalar", sa, l1)
            l2 = step("scalar", sa, lambda e: e.activation(rstd[:], rstd[:], AF.Exp, scale=-0.5))
            wait("vector", sa, l2)
            guard("vector", "negmr")
            nm = step("vector", sv, lambda e: e.tensor_scalar(
                negmr[:], mv[:, 0:1], rstd[:], -1.0, op0=ALU.mult, op1=ALU.mult))
            free("mv", sv, nm)
            return l2, nm

        dma("gpsimd", dc, ident_sb[:], ident_d)
        dma("gpsimd", dc, ind2_sb[:], ind2_d)
        const_ld = dc.n
        ones_set = step("vector", sv, lambda e: e.memset(ones1f[:], 1.0))

        p1 = contextlib.ExitStack()
        def sb1(name, shape, dt):
            return p1.enter_context(nc.sbuf_tensor(name, shape, dt))

        xa = sb1("xa", [128, 1024], F32)
        lnx_bf = sb1("lnx_bf", [128, 1024], BF16)
        lnxT_sb = sb1("lnxT_sb", [128, 4096], BF16)
        w1sb = sb1("w1sb", [128, 8 * 4096], BF16)
        w2sb = sb1("w2sb", [128, 32 * 1024], BF16)
        b1sb = sb1("b1sb", [128, 32], F32)
        b2sb = sb1("b2sb", [128, 8], F32)
        h_sb = sb1("h_sb", [128, 32 * 512], BF16)
        lt_sb2 = sb1("lt_sb2", [128, 2 * 8 * 512], BF16)
        ost = sb1("ost", [128, 8 * 512], BF16)
        sq_sb = sb1("sq_sb", [128, 2 * 512], BF16)

        p0cp = {}

        def p0_chunk(tc):
            par = tc % 2
            base = par * 8 * 512
            for m4 in range(4):
                tt = tc * 4 + m4
                guard("sync", "xa")
                xld = dma("sync", dy, xa[:], x_d[tt * 128:(tt + 1) * 128, :])
                wait("vector", dy, xld)
                _, _ = emit_ln_rstd(xa[:, 0:512], xa[:, 512:1024])
                guard("vector", "lnx_bf")
                nrm = step("vector", sv, lambda e: e.tensor_scalar(
                    lnx_bf[:], xa[:], mv[:, 0:1], rstd[:],
                    op0=ALU.subtract, op1=ALU.mult))
                free("xa", sv, nrm)
                guards["mv"] = (sv, nrm)
                guards["rstd"] = (sv, nrm)
                wait("tensor", sv, nrm)
                if tt == 0:
                    wait("tensor", dc, const_ld)
                for c in range(NC):
                    slot, skey = (psTa, "psTa") if c % 2 == 0 else (psTb, "psTb")
                    guard("tensor", skey)
                    tr = step("tensor", st_t, lambda e, c=c, s=slot: e.transpose(
                        s[:, 0:128], lnx_bf[:, c * 128:(c + 1) * 128], ident_sb[:]))
                    if c >= 6:
                        free("lnx_bf", st_t, tr)
                    wait("vector", st_t, tr)
                    if m4 == 0 and c == 0:
                        guard("vector", f"lt{par}")
                        if tc >= 2:
                            wait("vector", dls, dls.n)
                    cp = step("vector", sv, lambda e, c=c, m4=m4, s=slot, base=base: e.tensor_copy(
                        lt_sb2[:, base + c * 512 + m4 * 128: base + c * 512 + m4 * 128 + 128],
                        s[:, 0:128]))
                    free(skey, sv, cp)
            p0cp[tc] = sv.n
            wait("sync", sv, sv.n)
            for c in range(NC):
                dma("sync", dls, lnxT_s[c * 128:(c + 1) * 128, tc * 512:(tc + 1) * 512],
                    lt_sb2[:, base + c * 512:base + (c + 1) * 512])

        for tc in range(2 if stage >= 1 else TCN):
            p0_chunk(tc)

        if stage >= 1:
            wait("sync", dls, dls.n)

            def load_lt(gi, tc):
                par = gi % 2
                guard("sync", f"lt{par}")
                wait("sync", dls, dls.n)
                base = par * 8 * 512
                for c in range(NC):
                    dma("sync", dl, lt_sb2[:, base + c * 512: base + (c + 1) * 512],
                        lnxT_s[c * 128:(c + 1) * 128, tc * 512:(tc + 1) * 512])
                return dl.n

            gi_ld = 0
            gi_ld_counts = {}
            plan_chunks = [(p, tc) for p in ["k", "v", "q"][:p1_projs]
                           for tc in range(p1_tcs)]
            for pi, p in enumerate(["k", "v", "q"][:p1_projs]):
                guard("gpsimd", "w1sb")
                for kc in range(8):
                    dma("gpsimd", dg, w1sb[:, kc * 4096:(kc + 1) * 4096],
                        w1_d[p][kc * 128:(kc + 1) * 128, :])
                w1load = dg.n
                guard("gpsimd", "w2sb")
                for kc in range(32):
                    dma("gpsimd", dg2, w2sb[:, kc * 1024:(kc + 1) * 1024],
                        w2_d[p][kc * 128:(kc + 1) * 128, :])
                wload = dg2.n
                guard("sync", "b12")
                guard("sync", "b12_a")
                bl = dma("sync", db, b1sb[:], b1_d[p])
                bl = dma("sync", db, b2sb[:], b2_d[p])

                for tc in range(p1_tcs):
                    gi = pi * p1_tcs + tc
                    while gi_ld <= gi + 1 and gi_ld < len(plan_chunks):
                        if gi_ld < p1_tcs and stage >= 1:
                            gi_ld_counts[gi_ld] = dl.n
                        else:
                            gi_ld_counts[gi_ld] = load_lt(gi_ld, plan_chunks[gi_ld][1])
                        gi_ld += 1
                    par = gi % 2
                    lt_sb = lt_sb2[:, par * 8 * 512:(par + 1) * 8 * 512]
                    lt_ld = gi_ld_counts[min(gi + 1, len(plan_chunks) - 1)]
                    wait("tensor", dl, lt_ld)
                    if pi == 0:
                        wait("tensor", sv, p0cp[tc])
                    if tc == 0:
                        wait("tensor", dg, w1load)
                    for mc in range(32):
                        hkey = "psH0" if mc % 2 == 0 else "psH1"
                        half = psA if mc % 2 == 0 else psB
                        guard("tensor", hkey)
                        for kc in range(8):
                            ra = lt_sb[:, kc * 512:(kc + 1) * 512]
                            step("tensor", st_t, lambda e, mc=mc, kc=kc, h=half, ra=ra: e.matmul(
                                h[:, 0:512],
                                w1sb[:, kc * 4096 + mc * 128: kc * 4096 + mc * 128 + 128],
                                ra,
                                start=(kc == 0), stop=(kc == 7)),
                                k=(1 if kc == 7 else 0))
                        mmd = st_t.n
                        wait("scalar", st_t, mmd)
                        if tc == 0 and mc == 0:
                            wait("scalar", db, bl)
                        guard("scalar", f"h_sb{mc}")
                        ge = step("scalar", sa, lambda e, mc=mc, h=half: e.activation(
                            h_sb[:, mc * 512:(mc + 1) * 512], h[:, 0:512], AF.Gelu,
                            bias=b1sb[:, mc:mc + 1]))
                        free(hkey, sa, ge)
                    pend_norm = []
                    gelu_done = sa.n
                    rest("scalar")
                    if tc == 0:
                        wait("tensor", dg2, wload)
                    wait("tensor", sa, gelu_done)
                    guards[f"lt{par}"] = (st_t, st_t.n)
                    if pi == 0 and tc + 2 < TCN:
                        p0_chunk(tc + 2)
                    if tc == p1_tcs - 1 and p1_parts != 9:
                        guards["w1sb"] = (st_t, st_t.n)
                    for oc in range(NC if p1_parts >= 2 else 0):
                        okey = "psO0" if oc % 2 == 0 else "psO1"
                        if p1_parts == 6:
                            half = psO if oc % 2 == 0 else psM
                            oslc = (0, 512)
                        else:
                            half = psA if oc % 2 == 0 else psB
                            oslc = (512, 1024)
                        guard("tensor", okey)
                        kcn = 8 if p1_parts in (7, 8, 9) else 32
                        for kc in range(kcn):
                            if p1_parts == 8:
                                rhs_ap = lt_sb[:, kc * 512:(kc + 1) * 512]
                            else:
                                rhs_ap = h_sb[:, kc * 512:(kc + 1) * 512]
                            if p1_parts == 9:
                                lhs_ap = w1sb[:, kc * 4096 + oc * 128: kc * 4096 + oc * 128 + 128]
                            else:
                                lhs_ap = w2sb[:, kc * 1024 + oc * 128: kc * 1024 + oc * 128 + 128]
                            step("tensor", st_t, lambda e, h=half, s=oslc, kcn=kcn,
                                 kc=kc, la=lhs_ap, ra=rhs_ap: e.matmul(
                                h[:, s[0]:s[1]], la, ra,
                                start=(kc == 0), stop=(kc == kcn - 1)),
                                k=(1 if kc == kcn - 1 else 0))
                        omm = st_t.n
                        wait("vector", st_t, omm)
                        if tc == 0 and oc == 0:
                            wait("vector", db, bl)
                        guard("vector", f"ost{oc}")
                        if p1_parts == 4:
                            ev = step("vector", sv, lambda e, oc=oc, h=half, s=oslc: e.tensor_copy(
                                ost[:, oc * 512:(oc + 1) * 512], h[:, s[0]:s[1]]))
                        else:
                            ev = step("vector", sv, lambda e, oc=oc, h=half, s=oslc: e.tensor_scalar(
                                ost[:, oc * 512:(oc + 1) * 512], h[:, s[0]:s[1]],
                                b2sb[:, oc:oc + 1], None, op0=ALU.add))
                        free(okey, sv, ev)
                        if p in ("k", "q") and p1_parts >= 3:
                            sqs = sq_sb[:, (oc % 2) * 512:(oc % 2) * 512 + 512]
                            guard("vector", f"sq{oc % 2}")
                            wait("vector", sv, ev)
                            sq = step("vector", sv, lambda e, s=sqs, oc=oc: e.tensor_tensor(
                                s, ost[:, oc * 512:(oc + 1) * 512],
                                ost[:, oc * 512:(oc + 1) * 512], op=ALU.mult))
                            pend_norm.append((oc, sq, sqs))
                        if pend_norm and (pend_norm[0][1] <= sv.n - 1 or oc == NC - 1):
                            pass
                        while pend_norm and (oc > pend_norm[0][0] or oc == NC - 1):
                            poc, psq, psqs = pend_norm.pop(0)
                            wait("tensor", sv, psq)
                            guard("tensor", "psM")
                            nmm = step("tensor", st_t, lambda e, s=psqs: e.matmul(
                                psM[0:2, 0:512], ind2_sb[:], s, start=True, stop=True))
                            free(f"sq{poc % 2}", st_t, nmm)
                            wait("vector", st_t, nmm)
                            buf = nmax_k if p == "k" else nmax_q
                            idx = poc * 8 + tc
                            rd = step("vector", sv, lambda e, b=buf, i=idx: e.tensor_reduce(
                                b[0:2, i:i + 1], psM[0:2, 0:512], axis=AXX, op=ALU.max))
                            free("psM", sv, rd)
                    for mc in range(32):
                        guards[f"h_sb{mc}"] = (st_t, st_t.n)
                    wait("sync", sv, sv.n)
                    if p1_parts != 5:
                        for oc in range(NC):
                            dma("sync", dls, pT_s[p][oc * 128:(oc + 1) * 128,
                                                     tc * 512:(tc + 1) * 512],
                                ost[:, oc * 512:(oc + 1) * 512])
                        for oc in range(NC):
                            guards[f"ost{oc}"] = (dls, dls.n)
                guards["w2sb"] = (st_t, st_t.n)
                if p1_parts == 9:
                    guards["w1sb"] = (st_t, st_t.n)
                guards["b12"] = (sv, sv.n)
                guards["b12_a"] = (sa, sa.n)

        if stage < 3 and debug_scratch:
            dbg_d = nc.dram_tensor("dbg", [128, 4096], BF16,
                                   kind="ExternalOutput").ap()
            wait("sync", sv, sv.n)
            wait("sync", sa, sa.n)
            wait("sync", st_t, st_t.n)
            dma("sync", dy, dbg_d, lnxT_sb[:] if stage < 1 else lt_sb2[:, 0:4096])

        p1_t = step("tensor", st_t, lambda e: e.nop())
        p1_a = step("scalar", sa, lambda e: e.nop())
        p1_v = step("vector", sv, lambda e: e.nop())
        p1_dy = dy.n
        p1_dls = dls.n
        p1.close()

        if stage >= 2:
            p2 = contextlib.ExitStack()
            def sb2(name, shape, dt):
                return p2.enter_context(nc.sbuf_tensor(name, shape, dt))

            OT = sb2("OT", [128, 8 * 4096], BF16)
            projw_sb = sb2("projw_sb", [128, 8 * 1024], BF16)
            projb_bc = sb2("projb_bc", [128, 1024], BF16)
            winrep_sb = sb2("winrep_sb", [128, 8 * 1024], BF16)
            w2bd_sb = sb2("w2bd_sb", [128, 8 * 128], BF16)
            wout_sb = sb2("wout_sb", [64, 1024], BF16)
            bout_bc = sb2("bout_bc", [128, 1024], BF16)
            w1f_sb = sb2("w1f_sb", [128, 8], F32)
            binr_sb = sb2("binr_sb", [128, 8], F32)
            b1f_sb = sb2("b1f_sb", [128, 8], F32)
            hb_sb = sb2("hb_sb", [128, 8], F32)
            fb2_sb = sb2("fb2_sb", [64, 1], F32)
            certj = sb2("certj", [128, 1], BF16)
            p2a = contextlib.ExitStack()
            def sba(name, shape, dt):
                return p2a.enter_context(nc.sbuf_tensor(name, shape, dt))
            kT_2 = sba("kT_2", [128, 2 * 4096], BF16)
            vT_2 = sba("vT_2", [128, 2 * 4096], BF16)
            qT_p = sba("qT_p", [128, 4096], BF16)
            vh = {(pp, e2): sba(f"vh{pp}{e2}", [128, 32 * 65 + 64], BF16)
                  for pp in range(2) for e2 in range(2)}
            qz = {0: sba("qz0", [128, 4096], BF16),
                  1: sba("qz1", [128, 4096], BF16)}
            expS = sba("expS", [128, 4 * 1024], BF16)
            o_2 = sba("o_2", [64, 3 * 512], BF16)
            recip_sb = sba("recip_sb", [1, 3 * 512], F32)
            sums_row = sba("sums_row", [1, 3 * 512], F32)
            stage_row = sums_row
            xt_sb = None

            wait("gpsimd", st_t, p1_t)
            wait("gpsimd", sa, p1_a)
            wait("gpsimd", sv, p1_v)
            wait("gpsimd", dy, p1_dy)
            wait("gpsimd", dls, p1_dls)
            for kc in range(8):
                dma("gpsimd", dg, projw_sb[:, kc * 1024:(kc + 1) * 1024],
                    projw_d[kc * 128:(kc + 1) * 128, :])
            for kc in range(8):
                dma("gpsimd", dg, winrep_sb[:, kc * 1024:(kc + 1) * 1024],
                    winrep_d[kc * 128:(kc + 1) * 128, :])
            for kc in range(8):
                dma("gpsimd", dg, w2bd_sb[:, kc * 128:(kc + 1) * 128],
                    w2bd_d[kc * 128:(kc + 1) * 128, :])
            dma("gpsimd", dg, wout_sb[:], wout_d)
            w3_ld = dg.n

            wait("sync", st_t, p1_t)
            wait("sync", sa, p1_a)
            wait("sync", sv, p1_v)
            wait("sync", dls, p1_dls)
            pbl = dma("sync", ds, stage_row[0:1, 0:1024], projb_d)
            dma("sync", ds, w1f_sb[:], w1flat_d)
            dma("sync", ds, binr_sb[:], binrep_d)
            dma("sync", ds, b1f_sb[:], b1flat_d)
            dma("sync", ds, fb2_sb[:], fb2_d)
            setup_ld = ds.n

            wait("vector", sv, p1_v)
            km = step("vector", sv, lambda e: e.tensor_reduce(
                kmax8[0:2, 0:8],
                nmax_k[0:2, :].rearrange("p (c t) -> p c t", t=8),
                axis=AXX, op=ALU.max))
            wait("vector", sv, km)
            q3 = nmax_q[0:2, :].rearrange("p (c t) -> p c t", t=8)
            k3 = kmax8[0:2, 0:8].rearrange("p (c t) -> p c t", t=1)
            q3b, k3b = bass.broadcast_tensor_aps(q3, k3)
            p3o = prodm[0:2, :].rearrange("p (c t) -> p c t", t=8)
            pr_ = step("vector", sv, lambda e: e.tensor_tensor(p3o, q3b, k3b, op=ALU.mult))
            wait("scalar", sv, pr_)
            wait("scalar", sa, p1_a)
            lp = step("scalar", sa, lambda e: e.activation(negm[0:2, :], prodm[0:2, :], AF.Ln))
            wait("scalar", sa, lp)
            exm = step("scalar", sa, lambda e: e.activation(
                negm[0:2, :], negm[0:2, :], AF.Exp, scale=0.5))
            wait("vector", sa, exm)
            ng = step("vector", sv, lambda e: e.tensor_scalar(
                negm[0:2, :], negm[0:2, :], -0.125, None, op0=ALU.mult))
            wait("gpsimd", sv, ng)
            nr = dma("gpsimd", dn, negm_row[0:1, 0:128], negm[0:2, 0:64])
            wait("tensor", dn, nr)
            wait("tensor", sv, ones_set)
            wait("tensor", sa, p1_a)
            wait("tensor", sv, p1_v)
            guard("tensor", "psM")
            mbm = step("tensor", st_t, lambda e: e.matmul(
                psM[0:128, 0:128], ones1f[0:1, :], negm_row[0:1, :],
                start=True, stop=True))
            wait("vector", st_t, mbm)
            mbc = step("vector", sv, lambda e: e.tensor_copy(mbias[:], psM[0:128, 0:128]))
            free("psM", sv, mbc)

            wait("tensor", ds, setup_ld)
            for dst in (projb_bc, bout_bc):
                for half in range(2):
                    guard("tensor", "psM")
                    r1 = step("tensor", st_t, lambda e, h=half: e.matmul(
                        psM[:, 0:512], ones1f[0:1, :],
                        stage_row[0:1, h * 512:(h + 1) * 512], start=True, stop=True))
                    wait("vector", st_t, r1)
                    cpb = step("vector", sv, lambda e, d=dst, h=half: e.tensor_copy(
                        d[:, h * 512:(h + 1) * 512], psM[:, 0:512]))
                    free("psM", sv, cpb)
                if dst is projb_bc:
                    wait("sync", st_t, st_t.n)
                    bol = dma("sync", db, stage_row[0:1, 0:1024], bout_d)
                    wait("tensor", db, bol)
                else:
                    guards["sums_0"] = (st_t, st_t.n)
                    guards["sums_1"] = (st_t, st_t.n)
            wait("vector", ds, setup_ld)
            hb1 = step("vector", sv, lambda e: e.tensor_tensor(
                hb_sb[:], w1f_sb[:], binr_sb[:], op=ALU.mult))
            wait("vector", sv, hb1)
            hb2 = step("vector", sv, lambda e: e.tensor_tensor(
                hb_sb[:], hb_sb[:], b1f_sb[:], op=ALU.add))
            for key in vh:
                step("vector", sv, lambda e, b=vh[key]: e.memset(b[:], 1.0))
            for e_ in range(2):
                step("vector", sv, lambda e, b=qz[e_]: e.memset(b[:], 0.0))
            vh_ones = sv.n

            def warm_burst(n=20):
                guard("tensor", "psM")
                for _ in range(n):
                    op("tensor", lambda e: e.matmul(
                        psM[64:128, 0:512], ident_sb[0:64, 0:64],
                        projb_bc[0:64, 0:512], start=True, stop=True))

            def load_kv(c):
                par = c % 2
                guard("sync", f"kv{par}")
                if c >= 2:
                    wait("sync", dp, kv_counts[c - 1])

                dma("sync", dp, kT_2[:, par * 4096:(par + 1) * 4096],
                    pT_s["k"][c * 128:(c + 1) * 128, :])
                dma("sync", dp, vT_2[:, par * 4096:(par + 1) * 4096],
                    pT_s["v"][c * 128:(c + 1) * 128, :])
                return dp.n

            wait("sync", dls, p1_dls)
            rc_counts = {}
            scp_counts = {}
            meta = {}
            kv_counts = {}
            kv_counts[0] = load_kv(0)
            qld = 0
            for c in range(NC):
                par = c % 2
                kT_p = kT_2[:, par * 4096:(par + 1) * 4096]
                vT_p = vT_2[:, par * 4096:(par + 1) * 4096]
                guard("sync", "pairbuf")
                if c >= 1:
                    wait("sync", dq, qld)
                qld = dma("sync", dq, qT_p[:], pT_s["q"][c * 128:(c + 1) * 128, :])
                wait("tensor", dp, kv_counts[c])
                wait("tensor", dq, qld)
                if c == 0:
                    wait("tensor", sv, vh_ones)
                def emit_vtr(cc, e2, kc, first_guard=False):
                    slot, skey = (psTa, "psTa") if e2 == 0 else (psTb, "psTb")
                    b2 = e2 * 64
                    vTcc = vT_2[:, (cc % 2) * 4096:((cc % 2) + 1) * 4096]
                    guard("tensor", skey)
                    va = vTcc[b2:b2 + 64, kc * 128:(kc + 1) * 128]
                    ia = ident_sb[b2:b2 + 64, b2:b2 + 64]
                    tr = step("tensor", st_t, lambda e, s=slot, va=va, ia=ia: e.transpose(
                        s[:, 0:64], va, ia))
                    wait("vector", st_t, tr)
                    if first_guard:
                        guard("vector", f"vhfree{cc % 2}")
                    cp = step("vector", sv, lambda e, k2=(cc % 2, e2), kc=kc,
                              s=slot: e.tensor_copy(
                        vh[k2][:, kc * 65: kc * 65 + 64], s[:, 0:64]))
                    free(skey, sv, cp)
                    return tr

                tr_first = None
                if c == 0:
                    for e_ in range(2):
                        for kc in range(32):
                            tr = emit_vtr(0, e_, kc, first_guard=(e_ == 0 and kc == 0))
                            if tr_first is None:
                                tr_first = tr
                pend_vtr = [(c + 1, e2, kc) for e2 in range(2)
                            for kc in range(32)] if c + 1 < NC else []
                wait("vector", dq, qld)
                guard("vector", "qz")
                step("vector", sv, lambda e: e.tensor_copy(
                    qz[0][0:64, :], qT_p[0:64, :]))
                qzdone = step("vector", sv, lambda e: e.tensor_copy(
                    qz[1][64:128, :], qT_p[64:128, :]))
                vtr_done = sv.n
                if c + 1 < NC:
                    wait("sync", st_t, st_t.n if tr_first is None else tr_first)
                    kv_counts[c + 1] = load_kv(c + 1)
                wait("tensor", sv, vtr_done)
                wait("scalar", ds, setup_ld)
                wait("scalar", sv, mbc)

                for qc in range(TCN):
                    for e_ in range(2):
                        base = e_ * 64
                        midx = e_ * 64 + c * 8 + qc
                        it = (c * TCN + qc) * 2 + e_
                        crt_counts = {}
                        for kk in range(16):
                            sbank, bkey = (psA, "psSA") if kk % 2 == 0 else (psB, "psSB")
                            eslot = kk % 4
                            guard("tensor", bkey)
                            for j in range(2):
                                kc = kk * 2 + j
                                ka = kT_p[:, kc * 128:(kc + 1) * 128]
                                qa = qz[e_][:, qc * 512:(qc + 1) * 512]
                                step("tensor", st_t, lambda e, j=j, s=sbank,
                                     ka=ka, qa=qa: e.matmul(
                                    s[:, j * 512:(j + 1) * 512], ka, qa,
                                    start=True, stop=True), k=(1 if j == 1 else 0))
                            smm = st_t.n
                            if pend_vtr and qc >= 6:
                                if pend_vtr[0][1] == 0 and pend_vtr[0][2] == 0:
                                    wait("tensor", dp, kv_counts[c + 1])
                                cc2, e2, kc2 = pend_vtr.pop(0)
                                emit_vtr(cc2, e2, kc2,
                                         first_guard=(e2 == 0 and kc2 == 0))
                            if kk == 3 and it > 1:
                                par_p = (it - 2) % 3
                                guard("tensor", "psM")
                                wait("tensor", sv, rc_counts[par_p])
                                bcm = step("tensor", st_t, lambda e, pp=par_p: e.matmul(
                                    psM[0:64, 0:512], ones1f[0:1, 0:64],
                                    recip_sb[0:1, pp * 512:(pp + 1) * 512],
                                    start=True, stop=True))
                                free(f"rc_{par_p}", st_t, bcm)
                                bcm_prev = bcm
                            if kk > 0:
                                wait("tensor", sv, crt_counts[kk - 1])
                                if kk == 1:
                                    guard("tensor", "psO_attn")
                                for j in range(2):
                                    kc = (kk - 1) * 2 + j
                                    es = (kk - 1) % 4
                                    step("tensor", st_t, lambda e, kc=kc, j=j, es=es,
                                         k2=(c % 2, e_): e.matmul(
                                        psO[0:128, 0:512],
                                        vh[k2][:, kc * 65: kc * 65 + 128],
                                        expS[:, es * 1024 + j * 512:
                                             es * 1024 + (j + 1) * 512],
                                        start=(kc == 0), stop=(kc == 31)),
                                        k=(1 if j == 1 else 0))
                                free(f"expS{(kk - 1) % 4}", st_t, st_t.n)
                            wait("scalar", st_t, smm)
                            guard("scalar", f"expS{eslot}")
                            exq = step("scalar", sa, lambda e, s=sbank, es=eslot,
                                       m=midx: e.activation(
                                expS[:, es * 1024:(es + 1) * 1024], s[:, 0:1024],
                                AF.Exp, bias=mbias[:, m:m + 1], scale=0.125))
                            free(bkey, sa, exq)
                            wait("vector", sa, exq)
                            guard("vector", "certj")
                            crt_counts[kk] = step("vector", sv, lambda e, es=eslot: e.tensor_copy(
                                certj[:], expS[:, es * 1024 + 1023: es * 1024 + 1024]))
                            free("certj", sv, crt_counts[kk])
                            if kk == 6 and it > 0:
                                pr = (it - 1) % 3
                                wait("vector", sv, scp_counts[pr])
                                guard("vector", "rcscr")
                                guard("vector", f"rc_{pr}")
                                rc_counts[pr] = step("vector", sv, lambda e, pr=pr: e.reciprocal(
                                    recip_sb[0:1, pr * 512:(pr + 1) * 512],
                                    sums_row[0:1, pr * 512:(pr + 1) * 512]))
                                free("rcscr", sv, rc_counts[pr])
                                free(f"sums_{pr}", sv, rc_counts[pr])
                            if kk == 9 and it > 1:
                                par_p = (it - 2) % 3
                                wait("vector", st_t, bcm_prev)
                                pm = meta[(it - 2) % 3]
                                mul = step("vector", sv, lambda e, pm=pm, pp=par_p: e.tensor_tensor(
                                    OT[pm[0]:pm[0] + 64,
                                       pm[1] * 4096 + pm[2] * 512: pm[1] * 4096 + (pm[2] + 1) * 512],
                                    o_2[:, pp * 512:(pp + 1) * 512],
                                    psM[0:64, 0:512], op=ALU.mult))
                                free("psM", sv, mul)
                                free(f"o2_{par_p}", sv, mul)
                        wait("tensor", sv, crt_counts[15])
                        for j in range(2):
                            kc = 15 * 2 + j
                            step("tensor", st_t, lambda e, kc=kc, j=j, k2=(c % 2, e_): e.matmul(
                                psO[0:128, 0:512],
                                vh[k2][:, kc * 65: kc * 65 + 128],
                                expS[:, 3 * 1024 + j * 512: 3 * 1024 + (j + 1) * 512],
                                start=False, stop=(j == 1)), k=(1 if j == 1 else 0))
                        free("expS3", st_t, st_t.n)
                        pv_done = st_t.n
                        par = it % 3
                        wait("vector", st_t, pv_done)
                        guard("vector", f"o2_{par}")
                        ocp = step("vector", sv, lambda e, par=par: e.tensor_copy(
                            o_2[:, par * 512:(par + 1) * 512], psO[0:64, 0:512]))
                        guard("vector", f"sums_{par}")
                        scp = step("vector", sv, lambda e, par=par: e.tensor_copy(
                            sums_row[0:1, par * 512:(par + 1) * 512], psO[64:65, 0:512]))
                        free("psO_attn", sv, scp)
                        scp_counts[par] = scp
                        meta[it % 3] = (base, c, qc)
                assert not pend_vtr
                guards["pairbuf"] = (st_t, st_t.n)
                guards[f"vhfree{par}"] = (st_t, st_t.n)
                guards[f"kv{par}"] = (st_t, st_t.n)
                guards["qz"] = (st_t, st_t.n)

            n_it = NC * TCN * 2
            prf = (n_it - 1) % 3
            wait("vector", sv, scp_counts[prf])
            guard("vector", "rcscr")
            rc_counts[prf] = step("vector", sv, lambda e, pr=prf: e.reciprocal(
                recip_sb[0:1, pr * 512:(pr + 1) * 512],
                sums_row[0:1, pr * 512:(pr + 1) * 512]))
            free("rcscr", sv, rc_counts[prf])
            for itf in (n_it - 2, n_it - 1):
                par_p = itf % 3
                guard("tensor", "psM")
                wait("tensor", sv, rc_counts[par_p])
                bcm = step("tensor", st_t, lambda e, pp=par_p: e.matmul(
                    psM[0:64, 0:512], ones1f[0:1, 0:64],
                    recip_sb[0:1, pp * 512:(pp + 1) * 512], start=True, stop=True))
                wait("vector", st_t, bcm)
                pm = meta[itf % 3]
                mul = step("vector", sv, lambda e, pm=pm, pp=par_p: e.tensor_tensor(
                    OT[pm[0]:pm[0] + 64,
                       pm[1] * 4096 + pm[2] * 512: pm[1] * 4096 + (pm[2] + 1) * 512],
                    o_2[:, pp * 512:(pp + 1) * 512],
                    psM[0:64, 0:512], op=ALU.mult))
                free("psM", sv, mul)
                free(f"o2_{par_p}", sv, mul)

            attn_v = sv.n
            attn_t = st_t.n
            attn_a = sa.n
            attn_dp = dp.n
            p2a.close()

            xt_sb = sb2("xt_sb", [128, 1024], F32)
            attn_sb = sb2("attn_sb", [128, 1024], F32)
            x2_sb = sb2("x2_sb", [128, 8 * 1024], F32)
            ln3_bf = sb2("ln3_bf", [128, 1024], BF16)
            ln3T_sb = sb2("ln3T_sb", [128, 2 * 8 * 512], BF16)
            onesbf = sb2("onesbf", [1, 128], BF16)
            hf_sb = sb2("hf_sb", [128, 8 * 512], BF16)
            g_sb = sb2("g_sb", [64, 512], BF16)
            f_sb = sb2("f_sb", [128, 1024], F32)
            out_sb = sb2("out_sb", [128, 1024], F32)

            if stage >= 3:
                wait("tensor", sv, attn_v)
                wait("tensor", sa, attn_a)
                wait("tensor", dg, w3_ld)
                wait("scalar", st_t, attn_t)
                wait("scalar", sv, attn_v)
                wait("sync", st_t, attn_t)
                wait("sync", sv, attn_v)
                wait("sync", sa, attn_a)
                wait("sync", dp, attn_dp)
                out_count = 0
                pending = None
                prj = {}
                ob = step("vector", sv, lambda e: e.memset(onesbf[:], 1.0))
                wait("tensor", sv, ob)

                def proj_mms(t):
                    guard("tensor", "psA3")
                    for nn2 in range(2):
                        for kc in range(8):
                            step("tensor", st_t, lambda e, nn2=nn2, kc=kc, t=t: e.matmul(
                                psA[:, nn2 * 512:(nn2 + 1) * 512],
                                OT[:, kc * 4096 + t * 128: kc * 4096 + (t + 1) * 128],
                                projw_sb[:, kc * 1024 + nn2 * 512:
                                         kc * 1024 + (nn2 + 1) * 512],
                                start=(kc == 0), stop=False), k=0)
                        step("tensor", st_t, lambda e, nn2=nn2: e.matmul(
                            psA[:, nn2 * 512:(nn2 + 1) * 512],
                            onesbf[0:1, 0:128],
                            projb_bc[0:1, nn2 * 512:(nn2 + 1) * 512],
                            start=False, stop=True), k=(1 if nn2 == 1 else 0))
                    prj[t] = st_t.n

                def srep_pair(k, mis):
                    par = k % 2
                    if mis[0] == 0:
                        wait("tensor", sv, pending[1])
                    for mi in mis:
                        hh = mi % 2
                        guard("tensor", f"psF{hh}")
                        for kc in range(8):
                            step("tensor", st_t, lambda e, mi=mi, kc=kc, hh=hh, par=par: e.matmul(
                                psB[:, hh * 512:(hh + 1) * 512],
                                winrep_sb[:, kc * 1024 + mi * 128:
                                          kc * 1024 + mi * 128 + 128],
                                ln3T_sb[:, par * 4096 + kc * 512:
                                        par * 4096 + (kc + 1) * 512],
                                start=(kc == 0), stop=(kc == 7)),
                                k=(1 if kc == 7 else 0))
                    if mis[-1] == 7:
                        guards[f"ln3Tw{par}"] = (st_t, st_t.n)

                def srep_relu(k, mis):
                    if k == 0 and mis[0] == 0:
                        wait("scalar", ds, setup_ld)
                        wait("scalar", sv, hb2)
                    for mi in mis:
                        hh = mi % 2
                        wait("scalar", st_t, srep_done[mi])
                        guard("scalar", f"hf{mi}")
                        rl = step("scalar", sa, lambda e, mi=mi, hh=hh: e.activation(
                            hf_sb[:, mi * 512:(mi + 1) * 512],
                            psB[:, hh * 512:(hh + 1) * 512], AF.Relu,
                            scale=w1f_sb[:, mi:mi + 1], bias=hb_sb[:, mi:mi + 1]))
                        free(f"psF{hh}", sa, rl)
                    if mis[-1] == 7:
                        guards["relu_done"] = (sa, sa.n)

                def g_mms():
                    g = guards["relu_done"]
                    wait("tensor", g[0], g[1])
                    guard("tensor", "psOM4")
                    guard("tensor", "psO_g")
                    for mi in range(8):
                        step("tensor", st_t, lambda e, mi=mi: e.matmul(
                            psO[:, 0:512], w2bd_sb[:, mi * 128:(mi + 1) * 128],
                            hf_sb[:, mi * 512:(mi + 1) * 512],
                            start=(mi == 0), stop=(mi == 7)),
                            k=(1 if mi == 7 else 0))
                    gmm = st_t.n
                    for mi in range(8):
                        guards[f"hf{mi}"] = (st_t, gmm)
                    wait("scalar", st_t, gmm)
                    guard("scalar", "g_sb")
                    gv = step("scalar", sa, lambda e: e.activation(
                        g_sb[:], psO[0:64, 0:512], AF.Identity, bias=fb2_sb[:]))
                    free("psO_g", sa, gv)

                def g_cert():
                    wait("vector", sa, sa.n)
                    guard("vector", "certj")
                    gcrt = step("vector", sv, lambda e: e.tensor_copy(
                        certj[0:64, 0:1], g_sb[:, 511:512]))
                    free("certj", sv, gcrt)
                    guards["gcrt"] = (sv, gcrt)

                def wout_chain(k, j2):
                    nonlocal out_count
                    tt = k * 4 + j2
                    gc = guards["gcrt"]
                    wait("tensor", gc[0], gc[1])
                    guard("tensor", "psOM4")
                    for nn2 in range(2):
                        half = psO if nn2 == 0 else psM
                        step("tensor", st_t, lambda e, j2=j2, nn2=nn2, h=half: e.matmul(
                            h[:, 0:512],
                            g_sb[:, j2 * 128:(j2 + 1) * 128],
                            wout_sb[0:64, nn2 * 512:(nn2 + 1) * 512],
                            start=True, stop=False), k=0)
                        step("tensor", st_t, lambda e, nn2=nn2, h=half: e.matmul(
                            h[:, 0:512],
                            onesbf[0:1, 0:128],
                            bout_bc[0:1, nn2 * 512:(nn2 + 1) * 512],
                            start=False, stop=True), k=(1 if nn2 == 1 else 0))
                    fmm = st_t.n
                    if j2 == 3:
                        guards["g_sb"] = (st_t, fmm)
                    wait("scalar", st_t, fmm)
                    guard("scalar", "f_sb")
                    for nn2 in range(2):
                        half = psO if nn2 == 0 else psM
                        fs = step("scalar", sa, lambda e, nn2=nn2, h=half: e.activation(
                            f_sb[:, nn2 * 512:(nn2 + 1) * 512],
                            h[:, 0:512], AF.Identity))
                    free("psOM4", sa, fs)
                    wait("vector", sa, fs)
                    emit_ln_rstd(f_sb[:, 0:512], f_sb[:, 512:1024])
                    nm4 = sv.n
                    wait("scalar", sv, nm4)
                    guard("scalar", "attn_sb")
                    t4 = step("scalar", sa, lambda e: e.activation(
                        attn_sb[:], f_sb[:], AF.Identity,
                        scale=rstd[:, 0:1], bias=negmr[:, 0:1]))
                    free("f_sb", sa, t4)
                    guards["negmr"] = (sa, t4)
                    guards["rstd"] = (sa, t4)
                    wait("vector", sa, t4)
                    guard("vector", "out_sb")
                    slot = tt % 8
                    fin = step("vector", sv, lambda e, slot=slot: e.tensor_tensor(
                        out_sb[:], attn_sb[:],
                        x2_sb[:, slot * 1024:(slot + 1) * 1024], op=ALU.add))
                    guards["attn_sb"] = (sv, fin)
                    guards[f"x2_{slot}"] = (sv, fin)
                    wait("sync", sv, fin)
                    dma("sync", do, out_d[tt * 128:(tt + 1) * 128, :], out_sb[:])
                    out_count += 1
                    guards["out_sb"] = (do, out_count * 16)

                proj_mms(0)
                srep_done = {}
                for t in range(32):
                    c, j = divmod(t, 4)
                    ffn = pending is not None
                    guard("sync", "xt_sb")
                    xld = dma("sync", dy, xt_sb[:], x_d[t * 128:(t + 1) * 128, :])
                    wait("scalar", st_t, prj[t])
                    guard("scalar", "attn_sb")
                    for nn2 in range(2):
                        pa = step("scalar", sa, lambda e, nn2=nn2: e.activation(
                            attn_sb[:, nn2 * 512:(nn2 + 1) * 512],
                            psA[:, nn2 * 512:(nn2 + 1) * 512], AF.Identity))
                    free("psA3", sa, pa)
                    if t + 1 < 32:
                        proj_mms(t + 1)
                    wait("vector", sa, pa)
                    emit_ln_rstd(attn_sb[:, 0:512], attn_sb[:, 512:1024])
                    nm2 = sv.n
                    wait("scalar", sv, nm2)
                    guard("scalar", "f_sb")
                    t2 = step("scalar", sa, lambda e: e.activation(
                        f_sb[:], attn_sb[:], AF.Identity,
                        scale=rstd[:, 0:1], bias=negmr[:, 0:1]))
                    guards["negmr"] = (sa, t2)
                    guards["rstd"] = (sa, t2)
                    guards["attn_sb"] = (sa, t2)
                    if ffn:
                        if j == 0:
                            srep_pair(pending[0], (0, 1))
                            srep_done[0] = srep_done[1] = st_t.n
                            srep_relu(pending[0], (0, 1))
                        elif j == 1:
                            srep_pair(pending[0], (4, 5))
                            srep_done[4] = srep_done[5] = st_t.n
                            srep_relu(pending[0], (4, 5))
                        elif j == 2:
                            g_mms()
                    wait("vector", sa, t2)
                    wait("vector", dy, xld)
                    guard("vector", f"x2_{t % 8}")
                    x2w = step("vector", sv, lambda e, s=t % 8: e.tensor_tensor(
                        x2_sb[:, s * 1024:(s + 1) * 1024], f_sb[:],
                        xt_sb[:], op=ALU.add))
                    free("xt_sb", sv, x2w)
                    free("f_sb", sv, x2w)
                    wait("vector", sv, x2w)
                    _, nm3 = emit_ln_rstd(x2_sb[:, (t % 8) * 1024: (t % 8) * 1024 + 512],
                                 x2_sb[:, (t % 8) * 1024 + 512: (t % 8) * 1024 + 1024])
                    wait("vector", sv, nm3)
                    guard("vector", "ln3_bf")
                    n3 = step("vector", sv, lambda e, s=t % 8: e.tensor_scalar(
                        ln3_bf[:], x2_sb[:, s * 1024:(s + 1) * 1024],
                        rstd[:, 0:1], negmr[:, 0:1], op0=ALU.mult, op1=ALU.add))
                    guards["negmr"] = (sv, n3)
                    guards["rstd"] = (sv, n3)
                    wait("tensor", sv, n3)
                    for c2 in range(NC):
                        slot2, skey = (psTa, "psTa") if c2 % 2 == 0 else (psTb, "psTb")
                        guard("tensor", skey)
                        tr3 = step("tensor", st_t, lambda e, c2=c2, s=slot2: e.transpose(
                            s[:, 128:256], ln3_bf[:, c2 * 128:(c2 + 1) * 128],
                            ident_sb[:]))
                        if c2 >= 6:
                            free("ln3_bf", st_t, tr3)
                        wait("vector", st_t, tr3)
                        if j == 0 and c2 == 0:
                            guard("vector", f"ln3Tw{c % 2}")
                        cp3 = step("vector", sv, lambda e, c2=c2, j=j, s=slot2, par=c % 2: e.tensor_copy(
                            ln3T_sb[:, par * 4096 + c2 * 512 + j * 128:
                                    par * 4096 + c2 * 512 + j * 128 + 128],
                            s[:, 128:256]))
                        free(skey, sv, cp3)
                    if ffn:
                        if j == 0:
                            srep_pair(pending[0], (2, 3))
                            srep_done[2] = srep_done[3] = st_t.n
                            srep_relu(pending[0], (2, 3))
                        elif j == 1:
                            srep_pair(pending[0], (6, 7))
                            srep_done[6] = srep_done[7] = st_t.n
                            srep_relu(pending[0], (6, 7))
                        elif j == 2:
                            g_cert()
                            wout_chain(pending[0], 0)
                            wout_chain(pending[0], 1)
                        elif j == 3:
                            wout_chain(pending[0], 2)
                            wout_chain(pending[0], 3)
                    if j == 3:
                        pending = (c, sv.n)
                for pr_ in ((0, 1), (2, 3), (4, 5), (6, 7)):
                    srep_pair(pending[0], pr_)
                    for mi in pr_:
                        srep_done[mi] = st_t.n
                    srep_relu(pending[0], pr_)
                g_mms()
                g_cert()
                for j2 in range(4):
                    wout_chain(pending[0], j2)
                wait("sync", do, out_count * 16)
            p2.close()

        if stage < 3:
            wait("sync", st_t, st_t.n)
            wait("sync", sa, sa.n)
            wait("sync", sv, sv.n)
            for s_ in (dy, dg, dg2, dc, dn, db, dl, dls, dp, dq, ds):
                wait("sync", s_, s_.n)

        block = st_.enter_context(nc.Block())

        def replay(eng, lst):
            n = len(lst)
            for i, (kind, f, semh, cnt) in enumerate(lst):
                f(eng)
                if kind == "inc" and (i + 1 >= n or lst[i + 1][0] != "wait"):
                    eng.wait_ge(semh, cnt)

        @block.tensor
        def _(eng):
            replay(eng, ops["tensor"])

        @block.scalar
        def _(eng):
            replay(eng, ops["scalar"])

        @block.vector
        def _(eng):
            replay(eng, ops["vector"])

        @block.sync
        def _(eng):
            replay(eng, ops["sync"])

        @block.gpsimd
        def _(eng):
            replay(eng, ops["gpsimd"])

    return nc


def _host_common(kw):
    f = lambda a: np.ascontiguousarray(np.asarray(a, dtype=np.float32))
    pr = lambda b, k: np.ascontiguousarray(f(b).reshape(k, 128).T)
    common = {
        "qw1": f(kw["L0_attn_q_w1"]), "qw2": f(kw["L0_attn_q_w2"]),
        "kw1": f(kw["L0_attn_k_w1"]), "kw2": f(kw["L0_attn_k_w2"]),
        "vw1": f(kw["L0_attn_v_w1"]), "vw2": f(kw["L0_attn_v_w2"]),
        "qb1r": pr(kw["L0_attn_q_b1"], 32), "qb2r": pr(kw["L0_attn_q_b2"], 8),
        "kb1r": pr(kw["L0_attn_k_b1"], 32), "kb2r": pr(kw["L0_attn_k_b2"], 8),
        "vb1r": pr(kw["L0_attn_v_b1"], 32), "vb2r": pr(kw["L0_attn_v_b2"], 8),
        "projw": f(kw["L0_attn_proj_w"]),
        "projb": f(kw["L0_attn_proj_b"]).reshape(1, 1024),
        "winrep": np.ascontiguousarray(np.repeat(f(kw["L0_ffn_Win"]), 16, axis=1)),
        "w1flat": pr(np.asarray(kw["L0_ffn_W1"]).reshape(1024), 8),
        "binrep": pr(np.repeat(f(kw["L0_ffn_bin_"]), 16), 8),
        "b1flat": pr(np.asarray(kw["L0_ffn_b1"]).reshape(1024), 8),
        "fb2": f(kw["L0_ffn_b2"]).reshape(64, 1),
        "wout": f(kw["L0_ffn_Wout"]),
        "bout": f(kw["L0_ffn_bout"]).reshape(1, 1024),
        "ident": np.eye(128, dtype=np.float32).astype(ml_dtypes.bfloat16),
    }
    w2bd = np.zeros((1024, 128), np.float32)
    w2bd[np.arange(1024), np.arange(1024) // 16] = f(kw["L0_ffn_W2"]).reshape(1024)
    common["w2bd"] = w2bd
    ind2 = np.zeros((128, 2), np.float32)
    ind2[0:64, 0] = 1.0
    ind2[64:128, 1] = 1.0
    common["ind2"] = ind2.astype(ml_dtypes.bfloat16)
    return common


def kernel(L0_attn_x, **kw):
    X = np.ascontiguousarray(np.asarray(L0_attn_x, dtype=np.float32))
    common = _host_common(kw)
    nc = build()
    in_maps = [dict(common, x=np.ascontiguousarray(X[b])) for b in range(N_CORES)]
    res = run_bass_kernel_spmd(nc, in_maps, core_ids=list(range(N_CORES)))
    return np.stack([np.asarray(res.results[b]["out"]) for b in range(N_CORES)], axis=0)
